# version 10
# baseline (speedup 1.0000x reference)
import numpy as np
from contextlib import ExitStack

import concourse.bass as bass
import concourse.mybir as mybir
from concourse.bass_utils import run_bass_kernel_spmd

F32 = mybir.dt.float32
BF16 = mybir.dt.bfloat16
ALU = mybir.AluOpType
AF = mybir.ActivationFunctionType
AX = mybir.AxisListType

NBATCH = 4
T = 2048
D = 1024
I = 2816
H = 16
DH = 64
NT = T // 128
KD = D // 128
KI = I // 128
IC_CH = [512, 512, 384, 512, 512, 384]
IC_OFF = [0, 512, 1024, 1408, 1920, 2432]
HHW = 1408
TP_N = [4, 4, 3, 4, 4, 3]
TP_OFF = [0, 4, 8, 11, 15, 19]


def ap(t, off, dims):
    tt = t.tensor if hasattr(t, "tensor") else t
    return bass.AP(tt, off, dims)


class Sem:
    __slots__ = ("h", "v")

    def __init__(self, es, nc, name):
        self.h = es.enter_context(nc.semaphore(name))
        self.v = 0


class DS:

    def __init__(self, es, nc, name, nslots=1):
        self.s = [Sem(es, nc, f"{name}{i}") for i in range(nslots)]

    def issue(self, instr, slot=0):
        instr.then_inc(self.s[slot].h, 16)
        self.s[slot].v += 16
        return self.s[slot].v

    def wait_all(self, eng, slot=None):
        for s in (self.s if slot is None else [self.s[slot]]):
            eng.wait_ge(s.h, s.v)


def _quant(v, xin, xq_out, ta, cA, cB, scl2, rs2, F):
    nb = F // 16
    tin = xin.tensor if hasattr(xin, "tensor") else xin
    ps_in = xin.ap[0][0]
    xin3 = bass.AP(tin, xin.offset, [[ps_in, 128], [16, nb], [1, 16]])
    ta3 = ap(ta, ta.offset, [[ta.ap[0][0], 128], [16, nb], [1, 16]])
    cA3 = ap(cA, cA.offset, [[cA.ap[0][0], 128], [16, nb], [1, 16]])
    sclb = ap(scl2, scl2.offset, [[scl2.ap[0][0], 128], [1, nb], [0, 16]])
    rsb = ap(rs2, rs2.offset, [[rs2.ap[0][0], 128], [1, nb], [0, 16]])
    scl = ap(scl2, scl2.offset, [[scl2.ap[0][0], 128], [1, nb]])
    rs = ap(rs2, rs2.offset, [[rs2.ap[0][0], 128], [1, nb]])

    v.tensor_reduce(scl, xin3, AX.X, ALU.max, apply_absolute_value=True)
    v.drain()
    v.tensor_scalar(scl, scl, 1.0 / 6.0, 1e-12, ALU.mult, ALU.max)
    v.drain()
    v.reciprocal(rs, scl)
    v.drain()
    v.scalar_tensor_tensor(ta, xin, -1.0, xin, ALU.mult, ALU.max)
    v.drain()
    v.tensor_tensor(ta3, ta3, rsb, ALU.mult)
    v.drain()
    v.tensor_scalar(cA, ta, 0.25, None, ALU.is_gt)
    v.drain()
    v.scalar_tensor_tensor(cA, ta, 0.75, cA, ALU.is_gt, ALU.add)
    v.drain()
    v.scalar_tensor_tensor(cA, ta, 1.25, cA, ALU.is_gt, ALU.add)
    v.drain()
    v.scalar_tensor_tensor(cA, ta, 1.75, cA, ALU.is_gt, ALU.add)
    v.drain()
    v.tensor_scalar(cB, ta, 2.5, None, ALU.is_gt)
    v.drain()
    v.scalar_tensor_tensor(cB, ta, 3.5, cB, ALU.is_gt, ALU.add)
    v.drain()
    v.scalar_tensor_tensor(cA, cA, 0.5, cB, ALU.mult, ALU.add)
    v.drain()
    v.tensor_scalar(cB, ta, 5.0, 2.0, ALU.is_gt, ALU.mult)
    v.drain()
    v.tensor_tensor(cA, cA, cB, ALU.add)
    v.drain()
    v.tensor_tensor(cA3, cA3, sclb, ALU.mult)
    v.drain()
    v.tensor_scalar(cB, xin, 0.0, 2.0, ALU.is_ge, ALU.mult)
    v.drain()
    ins = v.scalar_tensor_tensor(xq_out, cB, 1.0, cA, ALU.subtract, ALU.mult)
    v.drain()
    return ins


def build_program():
    nc = bass.Bass("TRN2", target_bir_lowering=False)
    es = ExitStack()

    x_d = nc.dram_tensor("x", [NBATCH, T, D], F32, kind="ExternalInput").ap()
    Wq_d = nc.dram_tensor("Wq", [D, D], F32, kind="ExternalInput").ap()
    Wk_d = nc.dram_tensor("Wk", [D, D], F32, kind="ExternalInput").ap()
    Wv_d = nc.dram_tensor("Wv", [D, D], F32, kind="ExternalInput").ap()
    Wp_d = nc.dram_tensor("Wp", [D, D], F32, kind="ExternalInput").ap()
    wg_d = nc.dram_tensor("wg", [I, D], F32, kind="ExternalInput").ap()
    wu_d = nc.dram_tensor("wu", [I, D], F32, kind="ExternalInput").ap()
    wd_d = nc.dram_tensor("wd", [D, I], F32, kind="ExternalInput").ap()
    out_d = nc.dram_tensor("out", [NBATCH, T, D], F32, kind="ExternalOutput").ap()

    wgq_t = nc.dram_tensor("wgq_t", [D, I], BF16, kind="Internal").ap()
    wuq_t = nc.dram_tensor("wuq_t", [D, I], BF16, kind="Internal").ap()
    wdq_t = nc.dram_tensor("wdq_t", [I, D], BF16, kind="Internal").ap()
    xqT_d = nc.dram_tensor("xqT_d", [D, T], BF16, kind="Internal").ap()
    o_scr = nc.dram_tensor("o_scr", [H, T, DH], BF16, kind="Internal").ap()
    y_scr = nc.dram_tensor("y_scr", [2, D, D], F32, kind="Internal").ap()

    ident = es.enter_context(nc.sbuf_tensor("ident", [128, 128], BF16))
    mstrip = es.enter_context(nc.sbuf_tensor("mstrip", [128, 896], BF16))
    bnst = es.enter_context(nc.sbuf_tensor("bnst", [128, 12], F32))
    bnag = es.enter_context(nc.sbuf_tensor("bnag", [128, 2], F32))
    rstd = es.enter_context(nc.sbuf_tensor("rstd", [128, 1], F32))
    rcp = es.enter_context(nc.sbuf_tensor("rcp", [128, 1], F32))
    ta_s = es.enter_context(nc.sbuf_tensor("ta_s", [128, HHW], F32))
    cA_s = es.enter_context(nc.sbuf_tensor("cA_s", [128, HHW], F32))
    cB_s = es.enter_context(nc.sbuf_tensor("cB_s", [128, HHW], F32))
    scl_s = es.enter_context(nc.sbuf_tensor("scl_s", [128, HHW // 16], F32))
    rs_s = es.enter_context(nc.sbuf_tensor("rs_s", [128, HHW // 16], F32))

    spe = Sem(es, nc, "spe")
    sdv = Sem(es, nc, "sdv")
    sac = Sem(es, nc, "sac")
    wld = DS(es, nc, "wld", 2)
    wst = DS(es, nc, "wst", 1)
    axl = DS(es, nc, "axl", 2)
    ascr = DS(es, nc, "ascr", 1)
    bwl = DS(es, nc, "bwl", 1)
    csc = DS(es, nc, "csc", 1)
    dys = DS(es, nc, "dys", 2)
    ewl = DS(es, nc, "ewl", 1)
    exq = DS(es, nc, "exq", 2)
    exl = DS(es, nc, "exl", 2)
    eyl = DS(es, nc, "eyl", 2)
    eol = DS(es, nc, "eol", 2)
    ALL_DS = [wld, wst, axl, ascr, bwl, csc, dys, ewl, exq, exl, eyl, eol]

    def quant(v, xin, xq_out, F):
        return _quant(v, xin, xq_out, ta_s[:, 0:F],
                      cA_s[:, 0:F], cB_s[:, 0:F], scl_s[:], rs_s[:], F)

    with nc.Block() as b:
        @b.gpsimd
        def _(g):
            g.iota(ident[:], [[1, 128]], base=0, channel_multiplier=-1,
                   allow_small_or_imprecise_dtypes=True)
            g.drain()
            g.iota(mstrip[:], [[1, 896]], base=-384, channel_multiplier=-1,
                   allow_small_or_imprecise_dtypes=True)
            g.drain()
    with nc.Block() as b:
        @b.vector
        def _(v):
            v.tensor_scalar(ident[:], ident[:], 0.0, None, ALU.is_equal)
            v.drain()
            v.tensor_scalar(mstrip[:], mstrip[:], 0.0, None, ALU.is_ge)
            v.drain()

    WSPEC = [(wg_d, wgq_t, KI, D, KD), (wu_d, wuq_t, KI, D, KD),
             (wd_d, wdq_t, KD, I, KI)]
    with nc.sbuf_tensor("wtile", [128, 2 * I], F32) as wtile, \
         nc.sbuf_tensor("wqt", [128, 2 * I], BF16) as wqt, \
         nc.sbuf_tensor("wTimg", [128, KD * I], BF16) as wTimg, \
         nc.psum_tensor("ptW", [128, 2048], BF16) as ptW:
        for mi, (src, dst, R, Fq, KT_) in enumerate(WSPEC):
            G = (KT_ + 7) // 8
            grp_n = [min(8, KT_ - 8 * g) for g in range(G)]
            sdv0, spe0, sac0 = sdv.v, spe.v, sac.v
            w0 = [wld.s[0].v, wld.s[1].v]
            wst0 = wst.s[0].v
            q_done = lambda r, sdv0=sdv0: sdv0 + r + 1
            ev_done = lambda r, g, sac0=sac0, G=G: sac0 + G * r + g + 1
            tp_done = lambda r, g, spe0=spe0, G=G: spe0 + G * r + g + 1
            ld_done = lambda r, w0=w0: w0[r % 2] + 16 * (r // 2 + 1)

            with nc.Block() as b:
                @b.sync
                def _(sync, src=src, dst=dst, R=R, Fq=Fq, KT_=KT_, G=G,
                      q_done=q_done, ev_done=ev_done):
                    for r in range(R):
                        if r >= 2:
                            sync.wait_ge(sdv.h, q_done(r - 2))
                        wld.issue(sync.dma_start(
                            wtile[:, (r % 2) * I:(r % 2) * I + Fq],
                            src[r * 128:(r + 1) * 128, :]), r % 2)
                    sync.wait_ge(sac.h, ev_done(R - 1, G - 1))
                    wst.issue(sync.dma_start(
                        ap(dst.tensor, 0,
                           [[R * 128, 128], [128 * R * 128, KT_], [1, R * 128]]),
                        ap(wTimg, 0, [[KD * I, 128], [R * 128, KT_], [1, R * 128]])))

                @b.vector
                def _(v, R=R, Fq=Fq, G=G, ld_done=ld_done, tp_done=tp_done):
                    for r in range(R):
                        v.wait_ge(wld.s[r % 2].h, ld_done(r))
                        if r >= 2:
                            v.wait_ge(spe.h, tp_done(r - 2, G - 1))
                        xin = wtile[:, (r % 2) * I:(r % 2) * I + Fq]
                        wq_o = wqt[:, (r % 2) * I:(r % 2) * I + Fq]
                        if Fq <= HHW:
                            quant(v, xin, wq_o, Fq)
                        else:
                            hw = Fq // 2
                            quant(v, xin[:, 0:hw], wq_o[:, 0:hw], hw)
                            quant(v, xin[:, hw:Fq], wq_o[:, hw:Fq], hw)
                        v.nop().then_inc(sdv.h, 1)
                sdv.v += R

                @b.scalar
                def _(s, R=R, KT_=KT_, G=G, grp_n=grp_n, mi=mi,
                      tp_done=tp_done, wst0=wst0):
                    if mi > 0:
                        s.wait_ge(wst.s[0].h, wst0)
                    for r in range(R):
                        for g in range(G):
                            n = grp_n[g]
                            s.wait_ge(spe.h, tp_done(r, g))
                            bank = ((G * r + g) % 2) * 1024
                            s.copy(
                                ap(wTimg, (8 * g) * (R * 128) + r * 128,
                                   [[KD * I, 128], [R * 128, n], [1, 128]]),
                                ptW[:, bank:bank + n * 128]
                            ).then_inc(sac.h, 1)
                            s.drain()
                sac.v += G * R

                @b.tensor
                def _(t, R=R, KT_=KT_, G=G, grp_n=grp_n,
                      q_done=q_done, ev_done=ev_done):
                    for r in range(R):
                        t.wait_ge(sdv.h, q_done(r))
                        for g in range(G):
                            gidx = G * r + g
                            if gidx >= 2:
                                pr, pg_ = divmod(gidx - 2, G)
                                t.wait_ge(sac.h, ev_done(pr, pg_))
                            bank = (gidx % 2) * 1024
                            for kk in range(grp_n[g]):
                                kc = 8 * g + kk
                                ins = t.transpose(
                                    ptW[:, bank + kk * 128:bank + (kk + 1) * 128],
                                    wqt[:, (r % 2) * I + kc * 128:
                                        (r % 2) * I + (kc + 1) * 128],
                                    ident[:])
                            ins.then_inc(spe.h, 1)
                spe.v += G * R

    for bi in range(NBATCH):
        with nc.sbuf_tensor(f"xnTZ{bi}", [128, KD * T], BF16) as xnTZ, \
             nc.sbuf_tensor(f"QT{bi}", [128, KD * T], BF16) as QTb, \
             nc.sbuf_tensor(f"KT{bi}", [128, KD * T], BF16) as KTb, \
             nc.sbuf_tensor(f"Vb{bi}", [128, H * NT * 65], BF16) as Vb:

            with nc.sbuf_tensor(f"xt{bi}", [128, 2 * D], F32) as xt2, \
                 nc.sbuf_tensor(f"xn{bi}", [128, D], F32) as xnt, \
                 nc.sbuf_tensor(f"xnb{bi}", [128, D], BF16) as xnb, \
                 nc.sbuf_tensor(f"xqb{bi}", [128, D], BF16) as xqb, \
                 nc.sbuf_tensor(f"wbuf{bi}", [128, KD * D], BF16) as wbuf, \
                 nc.psum_tensor(f"ptA{bi}", [128, 2048], BF16) as ptA, \
                 nc.psum_tensor(f"pq{bi}", [128, 1024], F32) as pq, \
                 nc.Block() as b:
                sdvA, sacA, speA = sdv.v, sac.v, spe.v
                a0 = [axl.s[0].v, axl.s[1].v]
                as0 = ascr.s[0].v
                bw0 = bwl.s[0].v
                a_ld = lambda st: a0[st % 2] + 16 * (st // 2 + 1)
                b_wld = lambda wi: bw0 + 16 * (wi + 1)
                peo = []
                for st in range(NT):
                    peo.append(("tpA", st, 0))
                    if st % 4 == 3:
                        peo += [("q", oc, st // 4) for oc in range(KD)]
                peo += [("k", oc, tc) for tc in range(4) for oc in range(KD)]
                peo += [("v", st, nc2) for st in range(NT) for nc2 in range(2)]
                ppos = {t: i + 1 for i, t in enumerate(peo)}
                a_tp = lambda st: speA + ppos[("tpA", st, 0)]
                b_q = lambda oc, tc: speA + ppos[("q", oc, tc)]
                b_k = lambda oc, tc: speA + ppos[("k", oc, tc)]
                b_v = lambda st, nc2: speA + ppos[("v", st, nc2)]
                aco = []
                for st in range(NT):
                    aco += [("rst", st, 0), ("xnb", st, 0), ("aev", st, 0)]
                    if st % 4 == 3:
                        aco += [("qe", oc, st // 4) for oc in range(KD)]
                aco += [("ke", oc, tc) for tc in range(4) for oc in range(KD)]
                apos = {t: i + 1 for i, t in enumerate(aco)}
                a_rst = lambda st: sacA + apos[("rst", st, 0)]
                a_xnb = lambda st: sacA + apos[("xnb", st, 0)]
                a_ev = lambda st: sacA + apos[("aev", st, 0)]
                b_qe = lambda oc, tc: sacA + apos[("qe", oc, tc)]
                b_ke = lambda oc, tc: sacA + apos[("ke", oc, tc)]
                a_i1 = lambda st: sdvA + 3 * st + 1
                a_ixn = lambda st: sdvA + 3 * st + 2
                a_i2 = lambda st: sdvA + 3 * st + 3
                b_ve = lambda gi: sdvA + 48 + gi + 1
                def pq_prev_done(t, gbi):
                    if gbi < 2:
                        return
                    j = gbi - 2
                    if j < 32:
                        t.wait_ge(sac.h, b_qe(j % 8, j // 8))
                    elif j < 64:
                        j -= 32
                        t.wait_ge(sac.h, b_ke(j % 8, j // 8))
                    else:
                        t.wait_ge(sdv.h, b_ve(j - 64))
                fence_vals = [(s.h, s.v) for d in ALL_DS for s in d.s]

                @b.sync
                def _(sync):
                    for hh_, vv_ in fence_vals:
                        sync.wait_ge(hh_, vv_)
                    for st in range(NT):
                        if st >= 2:
                            sync.wait_ge(sdv.h, a_ixn(st - 2))
                        axl.issue(sync.dma_start(
                            xt2[:, (st % 2) * D:(st % 2 + 1) * D],
                            x_d[bi, st * 128:(st + 1) * 128, :]), st % 2)
                        sync.wait_ge(sac.h, a_ev(st))
                        ascr.issue(sync.dma_start(
                            ap(xqT_d.tensor, st * 128,
                               [[T, 128], [128 * T, KD], [1, 128]]),
                            ap(xqb, 0, [[D, 128], [128, KD], [1, 128]])))

                @b.gpsimd
                def _(g):
                    for hh_, vv_ in fence_vals:
                        g.wait_ge(hh_, vv_)
                    for wi, wsrc in enumerate((Wq_d, Wk_d, Wv_d)):
                        if wi == 1:
                            g.wait_ge(spe.h, b_q(KD - 1, 3))
                        elif wi == 2:
                            g.wait_ge(spe.h, b_k(KD - 1, 3))
                        bwl.issue(g.dma_start(
                            ap(wbuf, 0, [[KD * D, 128], [D, KD], [1, D]]),
                            ap(wsrc.tensor, 0, [[D, 128], [128 * D, KD], [1, D]])))

                @b.scalar
                def _(s):
                    for task in aco:
                        kind = task[0]
                        if kind == "rst":
                            st = task[1]
                            s.wait_ge(sdv.h, a_i1(st))
                            if st >= 1:
                                s.wait_ge(sdv.h, a_ixn(st - 1))
                            s.activation(rstd[:], bnag[:, 1:2], AF.Ln)
                            s.drain()
                            s.activation(rstd[:], rstd[:], AF.Exp, scale=-0.5
                                         ).then_inc(sac.h, 1)
                            s.drain()
                        elif kind == "xnb":
                            st = task[1]
                            s.wait_ge(sdv.h, a_ixn(st))
                            if st >= 1:
                                s.wait_ge(spe.h, a_tp(st - 1))
                            s.copy(xnb[:], xnt[:]).then_inc(sac.h, 1)
                            s.drain()
                        elif kind == "aev":
                            st = task[1]
                            s.wait_ge(spe.h, a_tp(st))
                            if st >= 1:
                                s.wait_ge(ascr.s[0].h, as0 + 16 * st)
                            s.copy(
                                ap(xnTZ, st * 128, [[KD * T, 128], [T, KD], [1, 128]]),
                                ptA[:, 0:1024])
                            s.drain()
                            s.copy(xqb[:], ptA[:, 1024:2048]).then_inc(sac.h, 1)
                            s.drain()
                        elif kind == "qe":
                            _, oc, tc = task
                            gi = tc * 8 + oc
                            s.wait_ge(spe.h, b_q(oc, tc))
                            bank = (gi % 2) * 512
                            s.copy(
                                QTb[:, oc * T + tc * 512:oc * T + (tc + 1) * 512],
                                pq[:, bank:bank + 512]).then_inc(sac.h, 1)
                            s.drain()
                        else:
                            _, oc, tc = task
                            gi = 32 + tc * 8 + oc
                            s.wait_ge(spe.h, b_k(oc, tc))
                            bank = (gi % 2) * 512
                            s.copy(
                                KTb[:, oc * T + tc * 512:oc * T + (tc + 1) * 512],
                                pq[:, bank:bank + 512]).then_inc(sac.h, 1)
                            s.drain()

                @b.vector
                def _(v):
                    v.wait_ge(axl.s[0].h, a0[0])
                    for st in range(NT):
                        xt = xt2[:, (st % 2) * D:(st % 2 + 1) * D]
                        v.wait_ge(axl.s[st % 2].h, a_ld(st))
                        v.bn_stats(bnst[:, 0:6], xt[:, 0:512])
                        v.drain()
                        v.bn_stats(bnst[:, 6:12], xt[:, 512:1024])
                        v.drain()
                        v.bn_aggr(bnag[:], bnst[:])
                        v.drain()
                        v.tensor_scalar(bnag[:, 1:2], bnag[:, 1:2], 1e-5, None,
                                        ALU.add).then_inc(sdv.h, 1)
                        v.drain()
                        v.wait_ge(sac.h, a_rst(st))
                        v.tensor_scalar(xnt[:], xt, bnag[:, 0:1], rstd[:],
                                        ALU.subtract, ALU.mult
                                        ).then_inc(sdv.h, 1)
                        v.drain()
                        if st >= 1:
                            v.wait_ge(ascr.s[0].h, as0 + 16 * st)
                        quant(v, xnt[:], xqb[:], D)
                        v.nop().then_inc(sdv.h, 1)
                    for st in range(NT):
                        for nc2 in range(2):
                            gi = 64 + st * 2 + nc2
                            v.wait_ge(spe.h, b_v(st, nc2))
                            bank = (gi % 2) * 512
                            v.tensor_copy(
                                ap(Vb, ((nc2 * 8) * NT + st) * 65,
                                   [[H * NT * 65, 128], [NT * 65, 8], [1, 64]]),
                                pq[:, bank:bank + 512]).then_inc(sdv.h, 1)
                            v.drain()
                    v.memset(ap(Vb, 64, [[H * NT * 65, 128], [NT * 65, H],
                                         [65, NT], [1, 1]]), 1.0)
                    v.drain()

                @b.tensor
                def _(t):
                    for task in peo:
                        kind = task[0]
                        if kind == "tpA":
                            st = task[1]
                            t.wait_ge(sdv.h, a_i2(st))
                            t.wait_ge(sac.h, a_xnb(st))
                            if st >= 1:
                                t.wait_ge(sac.h, a_ev(st - 1))
                            for kc in range(KD):
                                t.transpose(ptA[:, kc * 128:(kc + 1) * 128],
                                            xnb[:, kc * 128:(kc + 1) * 128], ident[:])
                            for kc in range(KD):
                                ins = t.transpose(
                                    ptA[:, 1024 + kc * 128:1024 + (kc + 1) * 128],
                                    xqb[:, kc * 128:(kc + 1) * 128], ident[:])
                            ins.then_inc(spe.h, 1)
                        elif kind == "q":
                            _, oc, tc = task
                            gi = tc * 8 + oc
                            if gi == 0:
                                t.wait_ge(bwl.s[0].h, b_wld(0))
                            t.wait_ge(sac.h, a_ev(4 * tc + 3))
                            pq_prev_done(t, gi)
                            bank = (gi % 2) * 512
                            for kc in range(KD):
                                ins = t.matmul(
                                    pq[:, bank:bank + 512],
                                    wbuf[:, kc * D + oc * 128:kc * D + (oc + 1) * 128],
                                    xnTZ[:, kc * T + tc * 512:kc * T + (tc + 1) * 512],
                                    start=(kc == 0), stop=(kc == KD - 1))
                            ins.then_inc(spe.h, 1)
                        elif kind == "k":
                            _, oc, tc = task
                            gi = 32 + tc * 8 + oc
                            if gi == 32:
                                t.wait_ge(bwl.s[0].h, b_wld(1))
                            t.wait_ge(sac.h, a_ev(4 * tc + 3))
                            pq_prev_done(t, gi)
                            bank = (gi % 2) * 512
                            for kc in range(KD):
                                ins = t.matmul(
                                    pq[:, bank:bank + 512],
                                    wbuf[:, kc * D + oc * 128:kc * D + (oc + 1) * 128],
                                    xnTZ[:, kc * T + tc * 512:kc * T + (tc + 1) * 512],
                                    start=(kc == 0), stop=(kc == KD - 1))
                            ins.then_inc(spe.h, 1)
                        else:
                            _, st, nc2 = task
                            gi = 64 + st * 2 + nc2
                            if gi == 64:
                                t.wait_ge(bwl.s[0].h, b_wld(2))
                            t.wait_ge(sac.h, a_ev(st))
                            pq_prev_done(t, gi)
                            bank = (gi % 2) * 512
                            for kc in range(KD):
                                ins = t.matmul(
                                    pq[:, bank:bank + 512],
                                    xnTZ[:, kc * T + st * 128:kc * T + st * 128 + 128],
                                    wbuf[:, kc * D + nc2 * 512:kc * D + (nc2 + 1) * 512],
                                    start=(kc == 0), stop=(kc == KD - 1))
                            ins.then_inc(spe.h, 1)
                spe.v += NT + 96
                sac.v += 3 * NT + 64
                sdv.v += 3 * NT + 32

            with nc.sbuf_tensor(f"expT{bi}", [128, 28 * 512], BF16) as expT, \
                 nc.sbuf_tensor(f"OT{bi}", [128, T], BF16) as OTs, \
                 nc.sbuf_tensor(f"Osb{bi}", [128, D], BF16) as Osb, \
                 nc.psum_tensor(f"ps{bi}", [128, 2048], F32) as ps, \
                 nc.psum_tensor(f"pav{bi}", [128, 1024], F32) as pav, \
                 nc.psum_tensor(f"ptC{bi}", [128, 2048], BF16) as ptC, \
                 nc.Block() as b:
                speC, sacC, sdvC = spe.v, sac.v, sdv.v
                cs0 = csc.s[0].v
                as_end = ascr.s[0].v
                order = []
                for h in range(H):
                    order.append(("sc", h, 0, 0))
                    if h >= 1:
                        order.append(("av", h - 1, 3))
                        order += [("tp", h - 1, qt) for qt in range(0, 4)]
                    order.append(("sc", h, 1, 0))
                    order.append(("av", h, 0))
                    if h >= 1:
                        order += [("tp", h - 1, qt) for qt in range(4, 8)]
                    order.append(("sc", h, 1, 1))
                    order.append(("sc", h, 2, 0))
                    order.append(("av", h, 1))
                    if h >= 1:
                        order += [("tp", h - 1, qt) for qt in range(8, 12)]
                    order.append(("sc", h, 2, 1))
                    order.append(("sc", h, 2, 2))
                    order.append(("sc", h, 3, 0))
                    order.append(("av", h, 2))
                    if h >= 1:
                        order += [("tp", h - 1, qt) for qt in range(12, 16)]
                    order.append(("sc", h, 3, 1))
                    order.append(("sc", h, 3, 2))
                    order.append(("sc", h, 3, 3))
                order.append(("av", H - 1, 3))
                order += [("tp", H - 1, qt) for qt in range(NT)]
                pos = {t: i + 1 for i, t in enumerate(order)}
                sc_list = [t for t in order if t[0] == "sc"]
                sc_prev = {sc_list[i]: sc_list[i - 1] for i in range(1, len(sc_list))}
                dvor = []
                for h in range(H):
                    if h >= 1:
                        dvor.append(("avev", h - 1, 3))
                        dvor += [("oev", h - 1, qt) for qt in range(0, 4)]
                    dvor.append(("msk", h, 0))
                    dvor.append(("avev", h, 0))
                    if h >= 1:
                        dvor += [("oev", h - 1, qt) for qt in range(4, 8)]
                    dvor.append(("msk", h, 1))
                    dvor.append(("avev", h, 1))
                    if h >= 1:
                        dvor += [("oev", h - 1, qt) for qt in range(8, 12)]
                    dvor.append(("msk", h, 2))
                    dvor.append(("avev", h, 2))
                    if h >= 1:
                        dvor += [("oev", h - 1, qt) for qt in range(12, 16)]
                    dvor.append(("msk", h, 3))
                dvor.append(("avev", H - 1, 3))
                dvor += [("oev", H - 1, qt) for qt in range(NT)]
                dpos = {t: i + 1 for i, t in enumerate(dvor)}
                c_sc = lambda h, c, g: speC + pos[("sc", h, c, g)]
                c_av = lambda h, c: speC + pos[("av", h, c)]
                c_tp = lambda h, qt: speC + pos[("tp", h, qt)]
                c_exp = lambda h, c, g: sacC + 10 * h + c * (c + 1) // 2 + g + 1
                c_msk = lambda h, c: sdvC + dpos[("msk", h, c)]
                c_avev = lambda h, c: sdvC + dpos[("avev", h, c)]
                c_oev = lambda h, qt: sdvC + dpos[("oev", h, qt)]
                c_odma = lambda h: cs0 + 16 * (2 * h + 1)
                c_zld = lambda h: cs0 + 16 * (2 * h + 2)
                ktpos = lambda c, kt: ((c % 2) * 12 + kt) if kt < 12 else (12 + kt)

                @b.sync
                def _(sync):
                    for h in range(H):
                        sync.wait_ge(sdv.h, c_oev(h, NT - 1))
                        csc.issue(sync.dma_start(
                            ap(o_scr.tensor, h * T * DH,
                               [[64, 128], [128 * 64, NT], [1, 64]]),
                            ap(Osb, 0, [[D, 128], [64, NT], [1, 64]])))
                        sync.wait_ge(csc.s[0].h, c_odma(h))
                        csc.issue(sync.dma_start(
                            xnTZ[:, h * 1024:(h + 1) * 1024],
                            ap(o_scr.tensor, h * T * DH,
                               [[1024, 128], [1, 1024]])))

                @b.tensor
                def _(t):
                    for task in order:
                        kind = task[0]
                        if kind == "sc":
                            _, h, c, g = task
                            hb = (h % 2) * 64
                            ho = h // 2
                            prev = sc_prev.get(task)
                            if prev is not None:
                                t.wait_ge(sac.h, c_exp(prev[1], prev[2], prev[3]))
                            for i in range(4):
                                kt = g * 4 + i
                                ins = t.matmul(
                                    ps[:, i * 512:(i + 1) * 512],
                                    KTb[hb:hb + 64,
                                        ho * T + kt * 128:ho * T + (kt + 1) * 128],
                                    QTb[hb:hb + 64,
                                        ho * T + c * 512:ho * T + (c + 1) * 512],
                                    start=True, stop=True)
                            ins.then_inc(spe.h, 1)
                        elif kind == "av":
                            _, h, c = task
                            t.wait_ge(sdv.h, c_msk(h, c))
                            if c >= 2:
                                t.wait_ge(sdv.h, c_avev(h, c - 2))
                            elif h >= 1:
                                t.wait_ge(sdv.h, c_avev(h - 1, c + 2))
                            bank = (c % 2) * 512
                            for kt in range(4 * c + 4):
                                kp = ktpos(c, kt)
                                ins = t.matmul(
                                    pav[0:65, bank:bank + 512],
                                    Vb[:, (h * NT + kt) * 65:(h * NT + kt) * 65 + 65],
                                    expT[:, kp * 512:(kp + 1) * 512],
                                    start=(kt == 0), stop=(kt == 4 * c + 3),
                                    skip_group_check=True)
                            ins.then_inc(spe.h, 1)
                        else:
                            _, h, qt = task
                            if qt == 0:
                                t.wait_ge(sdv.h, c_avev(h, 3))
                            if qt >= 2:
                                t.wait_ge(sdv.h, c_oev(h, qt - 2))
                            elif h >= 1:
                                t.wait_ge(sdv.h, c_oev(h - 1, qt + 14))
                            ins = t.transpose(
                                ptC[:, (qt % 2) * 1024:(qt % 2) * 1024 + 65],
                                OTs[0:65, qt * 128:(qt + 1) * 128],
                                ident[0:65, 0:65])
                            ins.then_inc(spe.h, 1)
                spe.v += 30 * H

                @b.scalar
                def _(s):
                    s.wait_ge(ascr.s[0].h, as_end)
                    for h in range(H):
                        for c in range(4):
                            for g in range(c + 1):
                                s.wait_ge(spe.h, c_sc(h, c, g))
                                ecol = ktpos(c, 4 * g) * 512
                                s.activation(
                                    expT[:, ecol:ecol + 2048],
                                    ps[:], AF.Exp, scale=0.125
                                ).then_inc(sac.h, 1)
                                s.drain()
                sac.v += 10 * H

                @b.vector
                def _(v):
                    v.wait_ge(ascr.s[0].h, as_end)
                    for task in dvor:
                        kind = task[0]
                        if kind == "msk":
                            _, h, c = task
                            v.wait_ge(sac.h, c_exp(h, c, c))
                            mcol = ktpos(c, 4 * c) * 512
                            v.tensor_tensor(
                                expT[:, mcol:mcol + 2048],
                                expT[:, mcol:mcol + 2048],
                                ap(mstrip, 384, [[896, 128], [-128, 4], [1, 512]]),
                                ALU.mult).then_inc(sdv.h, 1)
                            v.drain()
                        elif kind == "avev":
                            _, h, c = task
                            v.wait_ge(spe.h, c_av(h, c))
                            bank = (c % 2) * 512
                            v.tensor_copy(
                                OTs[0:65, c * 512:(c + 1) * 512],
                                pav[0:65, bank:bank + 512]).then_inc(sdv.h, 1)
                            v.drain()
                        else:
                            _, h, qt = task
                            if qt == 0 and h >= 1:
                                v.wait_ge(csc.s[0].h, c_zld(h - 1))
                            v.wait_ge(spe.h, c_tp(h, qt))
                            po = (qt % 2) * 1024
                            v.reciprocal(rcp[:], ptC[:, po + 64:po + 65])
                            v.drain()
                            v.tensor_scalar(
                                Osb[:, qt * 64:(qt + 1) * 64],
                                ptC[:, po:po + 64], rcp[:], None, ALU.mult
                            ).then_inc(sdv.h, 1)
                            v.drain()
                sdv.v += 24 * H

            with nc.sbuf_tensor(f"ysb{bi}", [128, 1024], F32) as ysb, \
                 nc.sbuf_tensor(f"wbufD{bi}", [128, KD * D], BF16) as wbuf, \
                 nc.psum_tensor(f"pd{bi}", [128, 1024], F32) as pd, \
                 nc.Block() as b:
                speD, sdvD = spe.v, sdv.v
                dy0 = [dys.s[0].v, dys.s[1].v]
                bwD = bwl.s[0].v
                cs_end = csc.s[0].v
                d_mm = lambda gi: speD + gi + 1
                d_ev = lambda gi: sdvD + gi + 1
                d_ydma = lambda gi: dy0[gi % 2] + 16 * (gi // 2 + 1)

                @b.gpsimd
                def _(g):
                    bwl.issue(g.dma_start(
                        ap(wbuf, 0, [[KD * D, 128], [D, KD], [1, D]]),
                        ap(Wp_d.tensor, 0, [[D, 128], [128 * D, KD], [1, D]])))

                @b.tensor
                def _(t):
                    t.wait_ge(bwl.s[0].h, bwD + 16)
                    t.wait_ge(csc.s[0].h, cs_end)
                    gi = 0
                    for par in range(2):
                        for at in range(KD):
                            for nc2 in range(2):
                                if gi >= 2:
                                    t.wait_ge(sdv.h, d_ev(gi - 2))
                                bank = (gi % 2) * 512
                                for j in range(8):
                                    hz = par * 8 + j
                                    ins = t.matmul(
                                        pd[:, bank:bank + 512],
                                        xnTZ[:, hz * 1024 + at * 128:hz * 1024 + (at + 1) * 128],
                                        wbuf[:, j * D + nc2 * 512:j * D + (nc2 + 1) * 512],
                                        start=(j == 0), stop=(j == 7))
                                ins.then_inc(spe.h, 1)
                                gi += 1
                spe.v += 32

                @b.vector
                def _(v):
                    for gi in range(32):
                        nc2 = gi % 2
                        v.wait_ge(spe.h, d_mm(gi))
                        if gi >= 2:
                            v.wait_ge(dys.s[gi % 2].h, d_ydma(gi - 2))
                        bank = (gi % 2) * 512
                        v.tensor_copy(ysb[:, nc2 * 512:(nc2 + 1) * 512],
                                      pd[:, bank:bank + 512]).then_inc(sdv.h, 1)
                        v.drain()
                sdv.v += 32

                @b.sync
                def _(sync):
                    for gi in range(32):
                        par = gi // 16
                        at = (gi // 2) % 8
                        nc2 = gi % 2
                        sync.wait_ge(sdv.h, d_ev(gi))
                        dys.issue(sync.dma_start(
                            y_scr[par, at * 128:(at + 1) * 128,
                                  nc2 * 512:(nc2 + 1) * 512],
                            ysb[:, nc2 * 512:(nc2 + 1) * 512]), gi % 2)

        with nc.sbuf_tensor(f"wgE{bi}", [128, KD * I], BF16) as wgE, \
             nc.sbuf_tensor(f"wuE{bi}", [128, KD * I], BF16) as wuE, \
             nc.sbuf_tensor(f"wdE{bi}", [128, KI * D], BF16) as wdE, \
             nc.sbuf_tensor(f"xqld{bi}", [128, 2 * D], BF16) as xqld, \
             nc.sbuf_tensor(f"hh0{bi}", [128, HHW], F32) as hh0, \
             nc.sbuf_tensor(f"hh1{bi}", [128, HHW], F32) as hh1, \
             nc.sbuf_tensor(f"hq{bi}", [128, I], BF16) as hq, \
             nc.sbuf_tensor(f"hqT{bi}", [128, I], BF16) as hqT, \
             nc.sbuf_tensor(f"xtE{bi}", [128, 2 * D], F32) as xtE, \
             nc.sbuf_tensor(f"ytE{bi}", [128, 2 * D], F32) as ytE, \
             nc.sbuf_tensor(f"outE{bi}", [128, 2 * D], F32) as outE, \
             nc.psum_tensor(f"pg{bi}", [128, 1024], F32) as pg, \
             nc.psum_tensor(f"pu{bi}", [128, 1024], F32) as pu, \
             nc.psum_tensor(f"ptE{bi}", [128, 2048], BF16) as ptE, \
             nc.psum_tensor(f"pdn{bi}", [128, 1024], F32) as pdn, \
             nc.Block() as b:
            speE, sdvE, sacE = spe.v, sdv.v, sac.v
            ew0 = ewl.s[0].v
            xq0 = [exq.s[0].v, exq.s[1].v]
            xl0 = [exl.s[0].v, exl.s[1].v]
            ey0 = [eyl.s[0].v, eyl.s[1].v]
            eo0 = [eol.s[0].v, eol.s[1].v]
            peo = [("g", 0, i) for i in range(6)]
            for st in range(NT):
                if st < NT - 1:
                    peo += [("g", st + 1, i) for i in range(4)]
                peo += [("t", st, g) for g in range(6)]
                peo += [("d", st, n) for n in range(2)]
                if st < NT - 1:
                    peo += [("g", st + 1, i) for i in range(4, 6)]
            pepos = {t: i + 1 for i, t in enumerate(peo)}
            e_gup = lambda st, ici: speE + pepos[("g", st, ici)]
            e_tp = lambda st, g: speE + pepos[("t", st, g)]
            e_dn = lambda st, nc2: speE + pepos[("d", st, nc2)]
            aco = [("s", 0, i) for i in range(6)]
            for st in range(NT):
                if st < NT - 1:
                    aco += [("s", st + 1, i) for i in range(3)]
                aco += [("e", st, g) for g in range(6)]
                if st < NT - 1:
                    aco += [("s", st + 1, i) for i in range(3, 6)]
            acpos = {t: i + 1 for i, t in enumerate(aco)}
            e_silu = lambda st, ici: sacE + acpos[("s", st, ici)]
            e_hqev = lambda st, g: sacE + acpos[("e", st, g)]
            dvo = []
            for st in range(NT):
                dvo += [("h", st, 0), ("h", st, 1), ("h", st, 2), ("h", st, 3),
                        ("q", st, 0), ("h", st, 4), ("h", st, 5), ("q", st, 1)]
                if st >= 1:
                    dvo += [("f", st - 1, 0)]
            dvo += [("f", NT - 1, 0)]
            dvpos = {t: i + 1 for i, t in enumerate(dvo)}
            e_hev = lambda st, ici: sdvE + dvpos[("h", st, ici)]
            e_q = lambda st, half: sdvE + dvpos[("q", st, half)]
            e_fin = lambda st: sdvE + dvpos[("f", st, 0)]
            e_ld = lambda st: xq0[st % 2] + 16 * (st // 2 + 1)
            e_xld = lambda st: xl0[st % 2] + 16 * (st // 2 + 1)
            e_yld = lambda st: ey0[st % 2] + 16 * (st // 2 + 1)
            e_odma = lambda st: eo0[st % 2] + 16 * (st // 2 + 1)

            @b.sync
            def _(sync):
                sync.wait_ge(dys.s[0].h, dys.s[0].v)
                sync.wait_ge(dys.s[1].h, dys.s[1].v)
                for wi, (scr, img, KT_, Fw) in enumerate(
                        ((wgq_t, wgE, KD, I), (wuq_t, wuE, KD, I),
                         (wdq_t, wdE, KI, D))):
                    rowsz = KD * I if wi < 2 else KI * D
                    ewl.issue(sync.dma_start(
                        ap(img, 0, [[rowsz, 128], [Fw, KT_], [1, Fw]]),
                        ap(scr.tensor, 0, [[Fw, 128], [128 * Fw, KT_], [1, Fw]])))
                for st0 in (0, 1):
                    exq.issue(sync.dma_start(
                        xqld[:, (st0 % 2) * D:(st0 % 2 + 1) * D],
                        ap(xqT_d.tensor, st0 * 128,
                           [[T, 128], [128 * T, KD], [1, 128]])), st0 % 2)
                    exl.issue(sync.dma_start(
                        xtE[:, (st0 % 2) * D:(st0 % 2 + 1) * D],
                        x_d[bi, st0 * 128:(st0 + 1) * 128, :]), st0 % 2)
                sync.wait_ge(dys.s[0].h, dys.s[0].v)
                for st0 in (0, 1):
                    eyl.issue(sync.dma_start(
                        ytE[:, (st0 % 2) * D:(st0 % 2 + 1) * D],
                        ap(y_scr.tensor, st0 * 64 * D,
                           [[D, 64], [D * D, 2], [1, D]])), st0 % 2)
                for st in range(NT):
                    if st + 2 <= NT - 1:
                        sync.wait_ge(spe.h, e_gup(st, 5))
                        exq.issue(sync.dma_start(
                            xqld[:, (st % 2) * D:(st % 2 + 1) * D],
                            ap(xqT_d.tensor, (st + 2) * 128,
                               [[T, 128], [128 * T, KD], [1, 128]])), st % 2)
                    if st >= 2:
                        sync.wait_ge(sdv.h, e_fin(st - 2))
                        eol.issue(sync.dma_start(
                            out_d[bi, (st - 2) * 128:(st - 1) * 128, :],
                            outE[:, ((st - 2) % 2) * D:((st - 2) % 2 + 1) * D]),
                            st % 2)
                        eyl.issue(sync.dma_start(
                            ytE[:, (st % 2) * D:(st % 2 + 1) * D],
                            ap(y_scr.tensor, st * 64 * D,
                               [[D, 64], [D * D, 2], [1, D]])), st % 2)
                        exl.issue(sync.dma_start(
                            xtE[:, (st % 2) * D:(st % 2 + 1) * D],
                            x_d[bi, st * 128:(st + 1) * 128, :]), st % 2)
                for st in (NT - 2, NT - 1):
                    sync.wait_ge(sdv.h, e_fin(st))
                    eol.issue(sync.dma_start(
                        out_d[bi, st * 128:(st + 1) * 128, :],
                        outE[:, (st % 2) * D:(st % 2 + 1) * D]), st % 2)

            @b.tensor
            def _(t):
                t.wait_ge(ewl.s[0].h, ew0 + 48)
                for task in peo:
                    kind = task[0]
                    if kind == "g":
                        _, st, ici = task
                        icw = IC_CH[ici]
                        bank = (ici % 2) * 512
                        if ici == 0:
                            t.wait_ge(exq.s[st % 2].h, e_ld(st))
                        if ici >= 2:
                            t.wait_ge(sac.h, e_silu(st, ici - 2))
                            t.wait_ge(sdv.h, e_hev(st, ici - 2))
                        elif st >= 1:
                            t.wait_ge(sac.h, e_silu(st - 1, ici + 4))
                            t.wait_ge(sdv.h, e_hev(st - 1, ici + 4))
                        io = IC_OFF[ici]
                        for kc in range(KD):
                            t.matmul(
                                pg[:, bank:bank + icw],
                                xqld[:, (st % 2) * D + kc * 128:(st % 2) * D + (kc + 1) * 128],
                                wgE[:, kc * I + io:kc * I + io + icw],
                                start=(kc == 0), stop=(kc == KD - 1))
                        for kc in range(KD):
                            ins = t.matmul(
                                pu[:, bank:bank + icw],
                                xqld[:, (st % 2) * D + kc * 128:(st % 2) * D + (kc + 1) * 128],
                                wuE[:, kc * I + io:kc * I + io + icw],
                                start=(kc == 0), stop=(kc == KD - 1))
                        ins.then_inc(spe.h, 1)
                    elif kind == "t":
                        _, st, g = task
                        t.wait_ge(sdv.h, e_q(st, 0 if g < 3 else 1))
                        if g >= 2:
                            t.wait_ge(sac.h, e_hqev(st, g - 2))
                        elif st >= 1:
                            t.wait_ge(sac.h, e_hqev(st - 1, g + 4))
                        bank = (g % 2) * 1024
                        for kk in range(TP_N[g]):
                            blk = TP_OFF[g] + kk
                            ins = t.transpose(
                                ptE[:, bank + kk * 128:bank + (kk + 1) * 128],
                                hq[:, blk * 128:(blk + 1) * 128], ident[:])
                        ins.then_inc(spe.h, 1)
                    else:
                        _, st, nc2 = task
                        t.wait_ge(sac.h, e_hqev(st, 5))
                        if st >= 1 and nc2 == 0:
                            t.wait_ge(sdv.h, e_fin(st - 1))
                        bank = nc2 * 512
                        for kc2 in range(KI):
                            ins = t.matmul(
                                pdn[:, bank:bank + 512],
                                hqT[:, kc2 * 128:(kc2 + 1) * 128],
                                wdE[:, kc2 * D + nc2 * 512:kc2 * D + (nc2 + 1) * 512],
                                start=(kc2 == 0), stop=(kc2 == KI - 1))
                        ins.then_inc(spe.h, 1)
            spe.v += 14 * NT

            @b.scalar
            def _(s):
                for task in aco:
                    kind = task[0]
                    if kind == "s":
                        _, st, ici = task
                        icw = IC_CH[ici]
                        bank = (ici % 2) * 512
                        s.wait_ge(spe.h, e_gup(st, ici))
                        if ici == 0 and st >= 1:
                            s.wait_ge(sdv.h, e_q(st - 1, 0))
                        if ici == 3 and st >= 1:
                            s.wait_ge(sdv.h, e_q(st - 1, 1))
                        half = 1 if ici >= 3 else 0
                        hhh = hh1 if half else hh0
                        coff = IC_OFF[ici] - half * HHW
                        s.activation(hhh[:, coff:coff + icw],
                                     pg[:, bank:bank + icw], AF.Silu
                                     ).then_inc(sac.h, 1)
                        s.drain()
                    else:
                        _, st, g = task
                        s.wait_ge(spe.h, e_tp(st, g))
                        bank = (g % 2) * 1024
                        n = TP_N[g]
                        s.copy(
                            hqT[:, TP_OFF[g] * 128:TP_OFF[g] * 128 + n * 128],
                            ptE[:, bank:bank + n * 128]).then_inc(sac.h, 1)
                        s.drain()
            sac.v += 12 * NT

            @b.vector
            def _(v):
                def do_fin(st):
                    v.wait_ge(spe.h, e_dn(st, 1))
                    v.wait_ge(exl.s[st % 2].h, e_xld(st))
                    v.wait_ge(eyl.s[st % 2].h, e_yld(st))
                    if st >= 2:
                        v.wait_ge(eol.s[st % 2].h, e_odma(st - 2))
                    ob = (st % 2) * D
                    for nc2 in range(2):
                        bank = nc2 * 512
                        v.tensor_tensor(
                            outE[:, ob + nc2 * 512:ob + (nc2 + 1) * 512],
                            xtE[:, ob + nc2 * 512:ob + (nc2 + 1) * 512],
                            pdn[:, bank:bank + 512], ALU.add)
                        v.drain()
                        ins = v.tensor_tensor(
                            outE[:, ob + nc2 * 512:ob + (nc2 + 1) * 512],
                            outE[:, ob + nc2 * 512:ob + (nc2 + 1) * 512],
                            ytE[:, ob + nc2 * 512:ob + (nc2 + 1) * 512], ALU.add)
                        v.drain()
                    ins.then_inc(sdv.h, 1)

                for task in dvo:
                    kind = task[0]
                    if kind == "h":
                        _, st, ici = task
                        icw = IC_CH[ici]
                        bank = (ici % 2) * 512
                        v.wait_ge(sac.h, e_silu(st, ici))
                        half_ = 1 if ici >= 3 else 0
                        hhh = hh1 if half_ else hh0
                        coff = IC_OFF[ici] - half_ * HHW
                        v.tensor_tensor(hhh[:, coff:coff + icw],
                                        hhh[:, coff:coff + icw],
                                        pu[:, bank:bank + icw], ALU.mult
                                        ).then_inc(sdv.h, 1)
                        v.drain()
                    elif kind == "q":
                        _, st, half = task
                        if st >= 1:
                            v.wait_ge(spe.h, e_tp(st - 1, 2 if half == 0 else 5))
                        quant(v, (hh1 if half else hh0)[:],
                              hq[:, half * HHW:(half + 1) * HHW], HHW)
                        v.nop().then_inc(sdv.h, 1)
                    else:
                        do_fin(task[1])
            sdv.v += 9 * NT

    with nc.Block() as b:
        @b.sync
        def _(sync):
            for d in ALL_DS:
                for s in d.s:
                    sync.wait_ge(s.h, s.v)

    nc.finalize()
    es.close()
    return nc


_CACHE = {}


def kernel(L0_attn_x, L0_attn_Wq, L0_attn_Wk, L0_attn_Wv, L0_attn_Wp,
           L0_ffn_w_gate, L0_ffn_w_up, L0_ffn_w_down):
    if "nc" not in _CACHE:
        _CACHE["nc"] = build_program()
    nc = _CACHE["nc"]
    x = np.ascontiguousarray(np.asarray(L0_attn_x, dtype=np.float32))
    common = {
        "Wq": np.ascontiguousarray(np.asarray(L0_attn_Wq, np.float32)),
        "Wk": np.ascontiguousarray(np.asarray(L0_attn_Wk, np.float32)),
        "Wv": np.ascontiguousarray(np.asarray(L0_attn_Wv, np.float32)),
        "Wp": np.ascontiguousarray(np.asarray(L0_attn_Wp, np.float32)),
        "wg": np.ascontiguousarray(np.asarray(L0_ffn_w_gate, np.float32)),
        "wu": np.ascontiguousarray(np.asarray(L0_ffn_w_up, np.float32)),
        "wd": np.ascontiguousarray(np.asarray(L0_ffn_w_down, np.float32)),
    }
    in_maps = [dict(common, x=x[c * NBATCH:(c + 1) * NBATCH]) for c in range(8)]
    res = run_bass_kernel_spmd(nc, in_maps, core_ids=list(range(8)))
    return np.concatenate([r["out"] for r in res.results], axis=0)


# version 19 (final)
# speedup vs baseline: 1.1024x; 1.1024x over previous
import numpy as np
from contextlib import ExitStack

import concourse.bass as bass
import concourse.mybir as mybir
from concourse.bass_utils import run_bass_kernel_spmd

F32 = mybir.dt.float32
BF16 = mybir.dt.bfloat16
ALU = mybir.AluOpType
AF = mybir.ActivationFunctionType
AX = mybir.AxisListType

NBATCH = 4
T = 2048
D = 1024
I = 2816
H = 16
DH = 64
NT = T // 128
KD = D // 128
KI = I // 128
IC_CH = [512, 512, 384, 512, 512, 384]
IC_OFF = [0, 512, 1024, 1408, 1920, 2432]
HHW = 1408
TP_N = [4, 4, 3, 4, 4, 3]
TP_OFF = [0, 4, 8, 11, 15, 19]


def ap(t, off, dims):
    tt = t.tensor if hasattr(t, "tensor") else t
    return bass.AP(tt, off, dims)


class Sem:
    __slots__ = ("h", "v")

    def __init__(self, es, nc, name):
        self.h = es.enter_context(nc.semaphore(name))
        self.v = 0


class DS:

    def __init__(self, es, nc, name, nslots=1):
        self.s = [Sem(es, nc, f"{name}{i}") for i in range(nslots)]

    def issue(self, instr, slot=0):
        instr.then_inc(self.s[slot].h, 16)
        self.s[slot].v += 16
        return self.s[slot].v

    def wait_all(self, eng, slot=None):
        for s in (self.s if slot is None else [self.s[slot]]):
            eng.wait_ge(s.h, s.v)


def _quant(v, xin, xq_out, ta, cA, cB, scl2, rs2, F):
    nb = F // 16
    tin = xin.tensor if hasattr(xin, "tensor") else xin
    ps_in = xin.ap[0][0]
    xin3 = bass.AP(tin, xin.offset, [[ps_in, 128], [16, nb], [1, 16]])
    ta3 = ap(ta, ta.offset, [[ta.ap[0][0], 128], [16, nb], [1, 16]])
    cA3 = ap(cA, cA.offset, [[cA.ap[0][0], 128], [16, nb], [1, 16]])
    sclb = ap(scl2, scl2.offset, [[scl2.ap[0][0], 128], [1, nb], [0, 16]])
    rsb = ap(rs2, rs2.offset, [[rs2.ap[0][0], 128], [1, nb], [0, 16]])
    scl = ap(scl2, scl2.offset, [[scl2.ap[0][0], 128], [1, nb]])
    rs = ap(rs2, rs2.offset, [[rs2.ap[0][0], 128], [1, nb]])

    v.tensor_reduce(scl, xin3, AX.X, ALU.max, apply_absolute_value=True)
    v.tensor_scalar(scl, scl, 1.0 / 6.0, 1e-12, ALU.mult, ALU.max)
    v.reciprocal(rs, scl)
    v.scalar_tensor_tensor(ta, xin, -1.0, xin, ALU.mult, ALU.max)
    v.tensor_tensor(ta3, ta3, rsb, ALU.mult)
    v.tensor_scalar(cA, ta, 0.25, None, ALU.is_gt)
    v.scalar_tensor_tensor(cA, ta, 0.75, cA, ALU.is_gt, ALU.add)
    v.scalar_tensor_tensor(cA, ta, 1.25, cA, ALU.is_gt, ALU.add)
    v.scalar_tensor_tensor(cA, ta, 1.75, cA, ALU.is_gt, ALU.add)
    v.tensor_scalar(cB, ta, 2.5, None, ALU.is_gt)
    v.scalar_tensor_tensor(cB, ta, 3.5, cB, ALU.is_gt, ALU.add)
    v.scalar_tensor_tensor(cA, cA, 0.5, cB, ALU.mult, ALU.add)
    v.tensor_scalar(cB, ta, 5.0, 2.0, ALU.is_gt, ALU.mult)
    v.tensor_tensor(cA, cA, cB, ALU.add)
    v.tensor_tensor(cA3, cA3, sclb, ALU.mult)
    v.tensor_scalar(cB, xin, 0.0, 2.0, ALU.is_ge, ALU.mult)
    ins = v.scalar_tensor_tensor(xq_out, cB, 1.0, cA, ALU.subtract, ALU.mult)
    return ins


def build_program():
    nc = bass.Bass("TRN2", target_bir_lowering=False)
    es = ExitStack()

    x_d = nc.dram_tensor("x", [NBATCH, T, D], F32, kind="ExternalInput").ap()
    Wq_d = nc.dram_tensor("Wq", [D, D], F32, kind="ExternalInput").ap()
    Wk_d = nc.dram_tensor("Wk", [D, D], F32, kind="ExternalInput").ap()
    Wv_d = nc.dram_tensor("Wv", [D, D], F32, kind="ExternalInput").ap()
    Wp_d = nc.dram_tensor("Wp", [D, D], F32, kind="ExternalInput").ap()
    wg_d = nc.dram_tensor("wg", [I, D], F32, kind="ExternalInput").ap()
    wu_d = nc.dram_tensor("wu", [I, D], F32, kind="ExternalInput").ap()
    wd_d = nc.dram_tensor("wd", [D, I], F32, kind="ExternalInput").ap()
    out_d = nc.dram_tensor("out", [NBATCH, T, D], F32, kind="ExternalOutput").ap()

    wgq_t = nc.dram_tensor("wgq_t", [D, I], BF16, kind="Internal").ap()
    wuq_t = nc.dram_tensor("wuq_t", [D, I], BF16, kind="Internal").ap()
    wdq_t = nc.dram_tensor("wdq_t", [I, D], BF16, kind="Internal").ap()
    xqT_d = nc.dram_tensor("xqT_d", [D, T], BF16, kind="Internal").ap()
    o_scr = nc.dram_tensor("o_scr", [H, T, DH], BF16, kind="Internal").ap()
    y_scr = nc.dram_tensor("y_scr", [2, D, D], F32, kind="Internal").ap()

    ident = es.enter_context(nc.sbuf_tensor("ident", [128, 128], BF16))
    mstrip = es.enter_context(nc.sbuf_tensor("mstrip", [128, 896], BF16))
    bnst = es.enter_context(nc.sbuf_tensor("bnst", [128, 12], F32))
    bnag = es.enter_context(nc.sbuf_tensor("bnag", [128, 2], F32))
    rstd = es.enter_context(nc.sbuf_tensor("rstd", [128, 1], F32))
    rcp = es.enter_context(nc.sbuf_tensor("rcp", [128, 1], F32))
    ta_s = es.enter_context(nc.sbuf_tensor("ta_s", [128, HHW], F32))
    cA_s = es.enter_context(nc.sbuf_tensor("cA_s", [128, HHW], F32))
    cB_s = es.enter_context(nc.sbuf_tensor("cB_s", [128, HHW], F32))
    scl_s = es.enter_context(nc.sbuf_tensor("scl_s", [128, HHW // 16], F32))
    rs_s = es.enter_context(nc.sbuf_tensor("rs_s", [128, HHW // 16], F32))

    spe = Sem(es, nc, "spe")
    sdv = Sem(es, nc, "sdv")
    sac = Sem(es, nc, "sac")
    wld = DS(es, nc, "wld", 2)
    wst = DS(es, nc, "wst", 1)
    axl = DS(es, nc, "axl", 2)
    ascr = DS(es, nc, "ascr", 1)
    bwl = DS(es, nc, "bwl", 1)
    csc = DS(es, nc, "csc", 1)
    dys = DS(es, nc, "dys", 2)
    ewl = DS(es, nc, "ewl", 1)
    exq = DS(es, nc, "exq", 2)
    exl = DS(es, nc, "exl", 2)
    eyl = DS(es, nc, "eyl", 2)
    eol = DS(es, nc, "eol", 2)
    ALL_DS = [wld, wst, axl, ascr, bwl, csc, dys, ewl, exq, exl, eyl, eol]

    def quant(v, xin, xq_out, F):
        return _quant(v, xin, xq_out, ta_s[:, 0:F],
                      cA_s[:, 0:F], cB_s[:, 0:F], scl_s[:], rs_s[:], F)

    with nc.Block() as b:
        @b.gpsimd
        def _(g):
            g.iota(ident[:], [[1, 128]], base=0, channel_multiplier=-1,
                   allow_small_or_imprecise_dtypes=True)
            g.drain()
            g.iota(mstrip[:], [[1, 896]], base=-384, channel_multiplier=-1,
                   allow_small_or_imprecise_dtypes=True)
            g.drain()
    with nc.Block() as b:
        @b.vector
        def _(v):
            v.tensor_scalar(ident[:], ident[:], 0.0, None, ALU.is_equal)
            v.drain()
            v.tensor_scalar(mstrip[:], mstrip[:], 0.0, None, ALU.is_ge)
            v.drain()

    WSPEC = [(wg_d, wgq_t, KI, D, KD), (wu_d, wuq_t, KI, D, KD),
             (wd_d, wdq_t, KD, I, KI)]
    with nc.sbuf_tensor("wtile", [128, 2 * I], F32) as wtile, \
         nc.sbuf_tensor("wqt", [128, 2 * I], BF16) as wqt, \
         nc.sbuf_tensor("wTimg", [128, KD * I], BF16) as wTimg, \
         nc.psum_tensor("ptW", [128, 2048], BF16) as ptW:
        for mi, (src, dst, R, Fq, KT_) in enumerate(WSPEC):
            G = (KT_ + 7) // 8
            grp_n = [min(8, KT_ - 8 * g) for g in range(G)]
            sdv0, spe0, sac0 = sdv.v, spe.v, sac.v
            w0 = [wld.s[0].v, wld.s[1].v]
            wst0 = wst.s[0].v
            q_done = lambda r, sdv0=sdv0: sdv0 + r + 1
            ev_done = lambda r, g, sac0=sac0, G=G: sac0 + G * r + g + 1
            tp_done = lambda r, g, spe0=spe0, G=G: spe0 + G * r + g + 1
            ld_done = lambda r, w0=w0: w0[r % 2] + 16 * (r // 2 + 1)

            with nc.Block() as b:
                @b.sync
                def _(sync, src=src, dst=dst, R=R, Fq=Fq, KT_=KT_, G=G,
                      q_done=q_done, ev_done=ev_done):
                    for r in range(R):
                        if r >= 2:
                            sync.wait_ge(sdv.h, q_done(r - 2))
                        wld.issue(sync.dma_start(
                            wtile[:, (r % 2) * I:(r % 2) * I + Fq],
                            src[r * 128:(r + 1) * 128, :]), r % 2)
                    sync.wait_ge(sac.h, ev_done(R - 1, G - 1))
                    wst.issue(sync.dma_start(
                        ap(dst.tensor, 0,
                           [[R * 128, 128], [128 * R * 128, KT_], [1, R * 128]]),
                        ap(wTimg, 0, [[KD * I, 128], [R * 128, KT_], [1, R * 128]])))

                @b.vector
                def _(v, R=R, Fq=Fq, G=G, ld_done=ld_done, tp_done=tp_done):
                    for r in range(R):
                        v.wait_ge(wld.s[r % 2].h, ld_done(r))
                        if r >= 2:
                            v.wait_ge(spe.h, tp_done(r - 2, G - 1))
                        xin = wtile[:, (r % 2) * I:(r % 2) * I + Fq]
                        wq_o = wqt[:, (r % 2) * I:(r % 2) * I + Fq]
                        if Fq <= HHW:
                            quant(v, xin, wq_o, Fq)
                        else:
                            hw = Fq // 2
                            quant(v, xin[:, 0:hw], wq_o[:, 0:hw], hw)
                            v.drain()
                            quant(v, xin[:, hw:Fq], wq_o[:, hw:Fq], hw)
                        v.drain()
                        v.nop().then_inc(sdv.h, 1)
                sdv.v += R

                @b.scalar
                def _(s, R=R, KT_=KT_, G=G, grp_n=grp_n, mi=mi,
                      tp_done=tp_done, wst0=wst0):
                    if mi > 0:
                        s.wait_ge(wst.s[0].h, wst0)
                    for r in range(R):
                        for g in range(G):
                            n = grp_n[g]
                            s.wait_ge(spe.h, tp_done(r, g))
                            if r == R - 1 and g == G - 1:
                                s.drain()
                            bank = ((G * r + g) % 2) * 1024
                            s.copy(
                                ap(wTimg, (8 * g) * (R * 128) + r * 128,
                                   [[KD * I, 128], [R * 128, n], [1, 128]]),
                                ptW[:, bank:bank + n * 128]
                            ).then_inc(sac.h, 1)
                sac.v += G * R

                @b.tensor
                def _(t, R=R, KT_=KT_, G=G, grp_n=grp_n,
                      q_done=q_done, ev_done=ev_done):
                    for r in range(R):
                        t.wait_ge(sdv.h, q_done(r))
                        for g in range(G):
                            gidx = G * r + g
                            if gidx >= 2:
                                pr, pg_ = divmod(gidx - 2, G)
                                t.wait_ge(sac.h, ev_done(pr, pg_))
                            bank = (gidx % 2) * 1024
                            for kk in range(grp_n[g]):
                                kc = 8 * g + kk
                                ins = t.transpose(
                                    ptW[:, bank + kk * 128:bank + (kk + 1) * 128],
                                    wqt[:, (r % 2) * I + kc * 128:
                                        (r % 2) * I + (kc + 1) * 128],
                                    ident[:])
                            ins.then_inc(spe.h, 1)
                spe.v += G * R

    for bi in range(NBATCH):
        with nc.sbuf_tensor(f"xnTZ{bi}", [128, KD * T], BF16) as xnTZ, \
             nc.sbuf_tensor(f"QT{bi}", [128, KD * T], BF16) as QTb, \
             nc.sbuf_tensor(f"KT{bi}", [128, KD * T], BF16) as KTb, \
             nc.sbuf_tensor(f"Vb{bi}", [128, H * NT * 65], BF16) as Vb:

            with nc.sbuf_tensor(f"xt{bi}", [128, 2 * D], F32) as xt2, \
                 nc.sbuf_tensor(f"xn{bi}", [128, D], F32) as xnt, \
                 nc.sbuf_tensor(f"xnb{bi}", [128, D], BF16) as xnb, \
                 nc.sbuf_tensor(f"xqb{bi}", [128, D], BF16) as xqb, \
                 nc.sbuf_tensor(f"wbuf{bi}", [128, KD * D], BF16) as wbuf, \
                 nc.psum_tensor(f"ptA{bi}", [128, 2048], BF16) as ptA, \
                 nc.psum_tensor(f"pq{bi}", [128, 1024], F32) as pq, \
                 nc.Block() as b:
                sdvA, sacA, speA = sdv.v, sac.v, spe.v
                a0 = [axl.s[0].v, axl.s[1].v]
                as0 = ascr.s[0].v
                bw0 = bwl.s[0].v
                a_ld = lambda st: a0[st % 2] + 16 * (st // 2 + 1)
                b_wld = lambda wi: bw0 + 16 * (wi + 1)
                peo = []
                for st in range(NT):
                    peo.append(("tpA", st, 0))
                    if st % 4 == 3:
                        peo += [("q", oc, st // 4) for oc in range(KD)]
                peo += [("k", oc, tc) for tc in range(4) for oc in range(KD)]
                peo += [("v", st, nc2) for st in range(NT) for nc2 in range(2)]
                ppos = {t: i + 1 for i, t in enumerate(peo)}
                a_tp = lambda st: speA + ppos[("tpA", st, 0)]
                b_q = lambda oc, tc: speA + ppos[("q", oc, tc)]
                b_k = lambda oc, tc: speA + ppos[("k", oc, tc)]
                b_v = lambda st, nc2: speA + ppos[("v", st, nc2)]
                aco = []
                for st in range(NT):
                    aco += [("rst", st, 0), ("xnb", st, 0), ("aev", st, 0)]
                    if st % 4 == 3:
                        aco += [("qe", oc, st // 4) for oc in range(KD)]
                aco += [("ke", oc, tc) for tc in range(4) for oc in range(KD)]
                apos = {t: i + 1 for i, t in enumerate(aco)}
                a_rst = lambda st: sacA + apos[("rst", st, 0)]
                a_xnb = lambda st: sacA + apos[("xnb", st, 0)]
                a_ev = lambda st: sacA + apos[("aev", st, 0)]
                b_qe = lambda oc, tc: sacA + apos[("qe", oc, tc)]
                b_ke = lambda oc, tc: sacA + apos[("ke", oc, tc)]
                a_i1 = lambda st: sdvA + 3 * st + 1
                a_ixn = lambda st: sdvA + 3 * st + 2
                a_i2 = lambda st: sdvA + 3 * st + 3
                b_ve = lambda gi: sdvA + 48 + gi + 1
                def pq_prev_done(t, gbi):
                    if gbi < 2:
                        return
                    j = gbi - 2
                    if j < 32:
                        t.wait_ge(sac.h, b_qe(j % 8, j // 8))
                    elif j < 64:
                        j -= 32
                        t.wait_ge(sac.h, b_ke(j % 8, j // 8))
                    else:
                        t.wait_ge(sdv.h, b_ve(j - 64))
                fence_vals = [(s.h, s.v) for d in ALL_DS for s in d.s]

                @b.sync
                def _(sync):
                    for hh_, vv_ in fence_vals:
                        sync.wait_ge(hh_, vv_)
                    for st in range(NT):
                        if st >= 2:
                            sync.wait_ge(sdv.h, a_ixn(st - 2))
                        axl.issue(sync.dma_start(
                            xt2[:, (st % 2) * D:(st % 2 + 1) * D],
                            x_d[bi, st * 128:(st + 1) * 128, :]), st % 2)
                        sync.wait_ge(sac.h, a_ev(st))
                        ascr.issue(sync.dma_start(
                            ap(xqT_d.tensor, st * 128,
                               [[T, 128], [128 * T, KD], [1, 128]]),
                            ap(xqb, 0, [[D, 128], [128, KD], [1, 128]])))

                @b.gpsimd
                def _(g):
                    for hh_, vv_ in fence_vals:
                        g.wait_ge(hh_, vv_)
                    for wi, wsrc in enumerate((Wq_d, Wk_d, Wv_d)):
                        if wi == 1:
                            g.wait_ge(spe.h, b_q(KD - 1, 3))
                        elif wi == 2:
                            g.wait_ge(spe.h, b_k(KD - 1, 3))
                        bwl.issue(g.dma_start(
                            ap(wbuf, 0, [[KD * D, 128], [D, KD], [1, D]]),
                            ap(wsrc.tensor, 0, [[D, 128], [128 * D, KD], [1, D]])))

                @b.scalar
                def _(s):
                    for task in aco:
                        kind = task[0]
                        if kind == "rst":
                            st = task[1]
                            s.wait_ge(sdv.h, a_i1(st))
                            if st >= 1:
                                s.wait_ge(sdv.h, a_ixn(st - 1))
                            s.activation(rstd[:], bnag[:, 1:2], AF.Ln)
                            s.drain()
                            s.activation(rstd[:], rstd[:], AF.Exp, scale=-0.5
                                         ).then_inc(sac.h, 1)
                        elif kind == "xnb":
                            st = task[1]
                            s.wait_ge(sdv.h, a_ixn(st))
                            if st >= 1:
                                s.wait_ge(spe.h, a_tp(st - 1))
                            s.copy(xnb[:], xnt[:]).then_inc(sac.h, 1)
                        elif kind == "aev":
                            st = task[1]
                            s.wait_ge(spe.h, a_tp(st))
                            if st >= 1:
                                s.wait_ge(ascr.s[0].h, as0 + 16 * st)
                            s.copy(
                                ap(xnTZ, st * 128, [[KD * T, 128], [T, KD], [1, 128]]),
                                ptA[:, 0:1024])
                            s.drain()
                            s.copy(xqb[:], ptA[:, 1024:2048]).then_inc(sac.h, 1)
                        elif kind == "qe":
                            _, oc, tc = task
                            gi = tc * 8 + oc
                            s.wait_ge(spe.h, b_q(oc, tc))
                            bank = (gi % 2) * 512
                            s.copy(
                                QTb[:, oc * T + tc * 512:oc * T + (tc + 1) * 512],
                                pq[:, bank:bank + 512]).then_inc(sac.h, 1)
                        else:
                            _, oc, tc = task
                            gi = 32 + tc * 8 + oc
                            s.wait_ge(spe.h, b_k(oc, tc))
                            if oc == KD - 1 and tc == 3:
                                s.drain()
                            bank = (gi % 2) * 512
                            s.copy(
                                KTb[:, oc * T + tc * 512:oc * T + (tc + 1) * 512],
                                pq[:, bank:bank + 512]).then_inc(sac.h, 1)

                @b.vector
                def _(v):
                    v.wait_ge(axl.s[0].h, a0[0])
                    for st in range(NT):
                        xt = xt2[:, (st % 2) * D:(st % 2 + 1) * D]
                        v.wait_ge(axl.s[st % 2].h, a_ld(st))
                        v.bn_stats(bnst[:, 0:6], xt[:, 0:512])
                        v.bn_stats(bnst[:, 6:12], xt[:, 512:1024])
                        v.drain()
                        v.bn_aggr(bnag[:], bnst[:])
                        v.drain()
                        v.tensor_scalar(bnag[:, 1:2], bnag[:, 1:2], 1e-5, None,
                                        ALU.add).then_inc(sdv.h, 1)
                        v.wait_ge(sac.h, a_rst(st))
                        v.tensor_scalar(xnt[:], xt, bnag[:, 0:1], rstd[:],
                                        ALU.subtract, ALU.mult
                                        ).then_inc(sdv.h, 1)
                        if st >= 1:
                            v.wait_ge(ascr.s[0].h, as0 + 16 * st)
                        v.drain()
                        quant(v, xnt[:], xqb[:], D)
                        v.drain()
                        v.nop().then_inc(sdv.h, 1)
                    for st in range(NT):
                        for nc2 in range(2):
                            gi = 64 + st * 2 + nc2
                            v.wait_ge(spe.h, b_v(st, nc2))
                            bank = (gi % 2) * 512
                            v.tensor_copy(
                                ap(Vb, ((nc2 * 8) * NT + st) * 65,
                                   [[H * NT * 65, 128], [NT * 65, 8], [1, 64]]),
                                pq[:, bank:bank + 512]).then_inc(sdv.h, 1)
                    v.memset(ap(Vb, 64, [[H * NT * 65, 128], [NT * 65, H],
                                         [65, NT], [1, 1]]), 1.0)
                    v.drain()

                @b.tensor
                def _(t):
                    for task in peo:
                        kind = task[0]
                        if kind == "tpA":
                            st = task[1]
                            t.wait_ge(sdv.h, a_i2(st))
                            t.wait_ge(sac.h, a_xnb(st))
                            if st >= 1:
                                t.wait_ge(sac.h, a_ev(st - 1))
                            for kc in range(KD):
                                t.transpose(ptA[:, kc * 128:(kc + 1) * 128],
                                            xnb[:, kc * 128:(kc + 1) * 128], ident[:])
                            for kc in range(KD):
                                ins = t.transpose(
                                    ptA[:, 1024 + kc * 128:1024 + (kc + 1) * 128],
                                    xqb[:, kc * 128:(kc + 1) * 128], ident[:])
                            ins.then_inc(spe.h, 1)
                        elif kind == "q":
                            _, oc, tc = task
                            gi = tc * 8 + oc
                            if gi == 0:
                                t.wait_ge(bwl.s[0].h, b_wld(0))
                            t.wait_ge(sac.h, a_ev(4 * tc + 3))
                            pq_prev_done(t, gi)
                            bank = (gi % 2) * 512
                            for kc in range(KD):
                                ins = t.matmul(
                                    pq[:, bank:bank + 512],
                                    wbuf[:, kc * D + oc * 128:kc * D + (oc + 1) * 128],
                                    xnTZ[:, kc * T + tc * 512:kc * T + (tc + 1) * 512],
                                    start=(kc == 0), stop=(kc == KD - 1))
                            ins.then_inc(spe.h, 1)
                        elif kind == "k":
                            _, oc, tc = task
                            gi = 32 + tc * 8 + oc
                            if gi == 32:
                                t.wait_ge(bwl.s[0].h, b_wld(1))
                            t.wait_ge(sac.h, a_ev(4 * tc + 3))
                            pq_prev_done(t, gi)
                            bank = (gi % 2) * 512
                            for kc in range(KD):
                                ins = t.matmul(
                                    pq[:, bank:bank + 512],
                                    wbuf[:, kc * D + oc * 128:kc * D + (oc + 1) * 128],
                                    xnTZ[:, kc * T + tc * 512:kc * T + (tc + 1) * 512],
                                    start=(kc == 0), stop=(kc == KD - 1))
                            ins.then_inc(spe.h, 1)
                        else:
                            _, st, nc2 = task
                            gi = 64 + st * 2 + nc2
                            if gi == 64:
                                t.wait_ge(bwl.s[0].h, b_wld(2))
                            t.wait_ge(sac.h, a_ev(st))
                            pq_prev_done(t, gi)
                            bank = (gi % 2) * 512
                            for kc in range(KD):
                                ins = t.matmul(
                                    pq[:, bank:bank + 512],
                                    xnTZ[:, kc * T + st * 128:kc * T + st * 128 + 128],
                                    wbuf[:, kc * D + nc2 * 512:kc * D + (nc2 + 1) * 512],
                                    start=(kc == 0), stop=(kc == KD - 1))
                            ins.then_inc(spe.h, 1)
                spe.v += NT + 96
                sac.v += 3 * NT + 64
                sdv.v += 3 * NT + 32

            with nc.sbuf_tensor(f"expT{bi}", [128, 28 * 512], BF16) as expT, \
                 nc.sbuf_tensor(f"OT{bi}", [128, T], BF16) as OTs, \
                 nc.sbuf_tensor(f"Osb{bi}", [128, D], BF16) as Osb, \
                 nc.psum_tensor(f"ps{bi}", [128, 2048], F32) as ps, \
                 nc.psum_tensor(f"pav{bi}", [128, 1024], F32) as pav, \
                 nc.psum_tensor(f"ptC{bi}", [128, 2048], BF16) as ptC, \
                 nc.Block() as b:
                speC, sacC, sdvC = spe.v, sac.v, sdv.v
                cs0 = csc.s[0].v
                as_end = ascr.s[0].v
                order = []
                for h in range(H):
                    order.append(("sc", h, 0, 0))
                    if h >= 1:
                        order.append(("av", h - 1, 3))
                        order += [("tp", h - 1, qt) for qt in range(0, 4)]
                    order.append(("sc", h, 1, 0))
                    order.append(("av", h, 0))
                    if h >= 1:
                        order += [("tp", h - 1, qt) for qt in range(4, 8)]
                    order.append(("sc", h, 1, 1))
                    order.append(("sc", h, 2, 0))
                    order.append(("av", h, 1))
                    if h >= 1:
                        order += [("tp", h - 1, qt) for qt in range(8, 12)]
                    order.append(("sc", h, 2, 1))
                    order.append(("sc", h, 2, 2))
                    order.append(("sc", h, 3, 0))
                    order.append(("av", h, 2))
                    if h >= 1:
                        order += [("tp", h - 1, qt) for qt in range(12, 16)]
                    order.append(("sc", h, 3, 1))
                    order.append(("sc", h, 3, 2))
                    order.append(("sc", h, 3, 3))
                order.append(("av", H - 1, 3))
                order += [("tp", H - 1, qt) for qt in range(NT)]
                pos = {t: i + 1 for i, t in enumerate(order)}
                sc_list = [t for t in order if t[0] == "sc"]
                sc_prev = {sc_list[i]: sc_list[i - 1] for i in range(1, len(sc_list))}
                dvor = []
                for h in range(H):
                    if h >= 1:
                        dvor.append(("avev", h - 1, 3))
                        dvor += [("oev", h - 1, qt) for qt in range(0, 4)]
                    dvor.append(("msk", h, 0))
                    dvor.append(("avev", h, 0))
                    if h >= 1:
                        dvor += [("oev", h - 1, qt) for qt in range(4, 8)]
                    dvor.append(("msk", h, 1))
                    dvor.append(("avev", h, 1))
                    if h >= 1:
                        dvor += [("oev", h - 1, qt) for qt in range(8, 12)]
                    dvor.append(("msk", h, 2))
                    dvor.append(("avev", h, 2))
                    if h >= 1:
                        dvor += [("oev", h - 1, qt) for qt in range(12, 16)]
                    dvor.append(("msk", h, 3))
                dvor.append(("avev", H - 1, 3))
                dvor += [("oev", H - 1, qt) for qt in range(NT)]
                dpos = {t: i + 1 for i, t in enumerate(dvor)}
                c_sc = lambda h, c, g: speC + pos[("sc", h, c, g)]
                c_av = lambda h, c: speC + pos[("av", h, c)]
                c_tp = lambda h, qt: speC + pos[("tp", h, qt)]
                c_exp = lambda h, c, g: sacC + 10 * h + c * (c + 1) // 2 + g + 1
                c_msk = lambda h, c: sdvC + dpos[("msk", h, c)]
                c_avev = lambda h, c: sdvC + dpos[("avev", h, c)]
                c_oev = lambda h, qt: sdvC + dpos[("oev", h, qt)]
                c_odma = lambda h: cs0 + 16 * (2 * h + 1)
                c_zld = lambda h: cs0 + 16 * (2 * h + 2)
                ktpos = lambda c, kt: ((c % 2) * 12 + kt) if kt < 12 else (12 + kt)

                @b.sync
                def _(sync):
                    for h in range(H):
                        sync.wait_ge(sdv.h, c_oev(h, NT - 1))
                        csc.issue(sync.dma_start(
                            ap(o_scr.tensor, h * T * DH,
                               [[64, 128], [128 * 64, NT], [1, 64]]),
                            ap(Osb, 0, [[D, 128], [64, NT], [1, 64]])))
                        sync.wait_ge(csc.s[0].h, c_odma(h))
                        csc.issue(sync.dma_start(
                            xnTZ[:, h * 1024:(h + 1) * 1024],
                            ap(o_scr.tensor, h * T * DH,
                               [[1024, 128], [1, 1024]])))

                @b.tensor
                def _(t):
                    for task in order:
                        kind = task[0]
                        if kind == "sc":
                            _, h, c, g = task
                            hb = (h % 2) * 64
                            ho = h // 2
                            prev = sc_prev.get(task)
                            if prev is not None:
                                t.wait_ge(sac.h, c_exp(prev[1], prev[2], prev[3]))
                            for i in range(4):
                                kt = g * 4 + i
                                ins = t.matmul(
                                    ps[:, i * 512:(i + 1) * 512],
                                    KTb[hb:hb + 64,
                                        ho * T + kt * 128:ho * T + (kt + 1) * 128],
                                    QTb[hb:hb + 64,
                                        ho * T + c * 512:ho * T + (c + 1) * 512],
                                    start=True, stop=True)
                            ins.then_inc(spe.h, 1)
                        elif kind == "av":
                            _, h, c = task
                            t.wait_ge(sdv.h, c_msk(h, c))
                            if c >= 2:
                                t.wait_ge(sdv.h, c_avev(h, c - 2))
                            elif h >= 1:
                                t.wait_ge(sdv.h, c_avev(h - 1, c + 2))
                            bank = (c % 2) * 512
                            for kt in range(4 * c + 4):
                                kp = ktpos(c, kt)
                                ins = t.matmul(
                                    pav[0:65, bank:bank + 512],
                                    Vb[:, (h * NT + kt) * 65:(h * NT + kt) * 65 + 65],
                                    expT[:, kp * 512:(kp + 1) * 512],
                                    start=(kt == 0), stop=(kt == 4 * c + 3),
                                    skip_group_check=True)
                            ins.then_inc(spe.h, 1)
                        else:
                            _, h, qt = task
                            if qt == 0:
                                t.wait_ge(sdv.h, c_avev(h, 3))
                            if qt >= 2:
                                t.wait_ge(sdv.h, c_oev(h, qt - 2))
                            elif h >= 1:
                                t.wait_ge(sdv.h, c_oev(h - 1, qt + 14))
                            ins = t.transpose(
                                ptC[:, (qt % 2) * 1024:(qt % 2) * 1024 + 65],
                                OTs[0:65, qt * 128:(qt + 1) * 128],
                                ident[0:65, 0:65])
                            ins.then_inc(spe.h, 1)
                spe.v += 30 * H

                @b.scalar
                def _(s):
                    s.wait_ge(ascr.s[0].h, as_end)
                    for h in range(H):
                        for c in range(4):
                            for g in range(c + 1):
                                s.wait_ge(spe.h, c_sc(h, c, g))
                                if g == c and c >= 1:
                                    s.drain()
                                ecol = ktpos(c, 4 * g) * 512
                                s.activation(
                                    expT[:, ecol:ecol + 2048],
                                    ps[:], AF.Exp, scale=0.125
                                ).then_inc(sac.h, 1)
                sac.v += 10 * H

                @b.vector
                def _(v):
                    v.wait_ge(ascr.s[0].h, as_end)
                    for task in dvor:
                        kind = task[0]
                        if kind == "msk":
                            _, h, c = task
                            v.wait_ge(sac.h, c_exp(h, c, c))
                            mcol = ktpos(c, 4 * c) * 512
                            v.tensor_tensor(
                                expT[:, mcol:mcol + 2048],
                                expT[:, mcol:mcol + 2048],
                                ap(mstrip, 384, [[896, 128], [-128, 4], [1, 512]]),
                                ALU.mult).then_inc(sdv.h, 1)
                        elif kind == "avev":
                            _, h, c = task
                            v.wait_ge(spe.h, c_av(h, c))
                            if c == 3:
                                v.drain()
                            bank = (c % 2) * 512
                            v.tensor_copy(
                                OTs[0:65, c * 512:(c + 1) * 512],
                                pav[0:65, bank:bank + 512]).then_inc(sdv.h, 1)
                        else:
                            _, h, qt = task
                            if qt == 0 and h >= 1:
                                v.wait_ge(csc.s[0].h, c_zld(h - 1))
                            v.wait_ge(spe.h, c_tp(h, qt))
                            po = (qt % 2) * 1024
                            v.reciprocal(rcp[:], ptC[:, po + 64:po + 65])
                            v.drain()
                            v.tensor_scalar(
                                Osb[:, qt * 64:(qt + 1) * 64],
                                ptC[:, po:po + 64], rcp[:], None, ALU.mult
                            ).then_inc(sdv.h, 1)
                sdv.v += 24 * H

            with nc.sbuf_tensor(f"ysb{bi}", [128, 1024], F32) as ysb, \
                 nc.sbuf_tensor(f"wbufD{bi}", [128, KD * D], BF16) as wbuf, \
                 nc.psum_tensor(f"pd{bi}", [128, 1024], F32) as pd, \
                 nc.Block() as b:
                speD, sdvD = spe.v, sdv.v
                dy0 = [dys.s[0].v, dys.s[1].v]
                bwD = bwl.s[0].v
                cs_end = csc.s[0].v
                d_mm = lambda gi: speD + gi + 1
                d_ev = lambda gi: sdvD + gi + 1
                d_ydma = lambda gi: dy0[gi % 2] + 16 * (gi // 2 + 1)

                @b.gpsimd
                def _(g):
                    bwl.issue(g.dma_start(
                        ap(wbuf, 0, [[KD * D, 128], [D, KD], [1, D]]),
                        ap(Wp_d.tensor, 0, [[D, 128], [128 * D, KD], [1, D]])))

                @b.tensor
                def _(t):
                    t.wait_ge(bwl.s[0].h, bwD + 16)
                    t.wait_ge(csc.s[0].h, cs_end)
                    gi = 0
                    for par in range(2):
                        for at in range(KD):
                            for nc2 in range(2):
                                if gi >= 2:
                                    t.wait_ge(sdv.h, d_ev(gi - 2))
                                bank = (gi % 2) * 512
                                for j in range(8):
                                    hz = par * 8 + j
                                    ins = t.matmul(
                                        pd[:, bank:bank + 512],
                                        xnTZ[:, hz * 1024 + at * 128:hz * 1024 + (at + 1) * 128],
                                        wbuf[:, j * D + nc2 * 512:j * D + (nc2 + 1) * 512],
                                        start=(j == 0), stop=(j == 7))
                                ins.then_inc(spe.h, 1)
                                gi += 1
                spe.v += 32

                @b.vector
                def _(v):
                    for gi in range(32):
                        nc2 = gi % 2
                        v.wait_ge(spe.h, d_mm(gi))
                        if gi >= 2:
                            v.wait_ge(dys.s[gi % 2].h, d_ydma(gi - 2))
                        bank = (gi % 2) * 512
                        v.tensor_copy(ysb[:, nc2 * 512:(nc2 + 1) * 512],
                                      pd[:, bank:bank + 512]).then_inc(sdv.h, 1)
                sdv.v += 32

                @b.sync
                def _(sync):
                    for gi in range(32):
                        par = gi // 16
                        at = (gi // 2) % 8
                        nc2 = gi % 2
                        sync.wait_ge(sdv.h, d_ev(gi))
                        dys.issue(sync.dma_start(
                            y_scr[par, at * 128:(at + 1) * 128,
                                  nc2 * 512:(nc2 + 1) * 512],
                            ysb[:, nc2 * 512:(nc2 + 1) * 512]), gi % 2)

        with nc.sbuf_tensor(f"wgE{bi}", [128, KD * I], BF16) as wgE, \
             nc.sbuf_tensor(f"wuE{bi}", [128, KD * I], BF16) as wuE, \
             nc.sbuf_tensor(f"wdE{bi}", [128, KI * D], BF16) as wdE, \
             nc.sbuf_tensor(f"xqld{bi}", [128, 2 * D], BF16) as xqld, \
             nc.sbuf_tensor(f"hh0{bi}", [128, HHW], F32) as hh0, \
             nc.sbuf_tensor(f"hh1{bi}", [128, HHW], F32) as hh1, \
             nc.sbuf_tensor(f"hq{bi}", [128, I], BF16) as hq, \
             nc.sbuf_tensor(f"hqT{bi}", [128, I], BF16) as hqT, \
             nc.sbuf_tensor(f"xtE{bi}", [128, 2 * D], F32) as xtE, \
             nc.sbuf_tensor(f"ytE{bi}", [128, 2 * D], F32) as ytE, \
             nc.sbuf_tensor(f"outE{bi}", [128, 2 * D], F32) as outE, \
             nc.psum_tensor(f"pg{bi}", [128, 1024], F32) as pg, \
             nc.psum_tensor(f"pu{bi}", [128, 1024], F32) as pu, \
             nc.psum_tensor(f"ptE{bi}", [128, 2048], BF16) as ptE, \
             nc.psum_tensor(f"pdn{bi}", [128, 1024], F32) as pdn, \
             nc.Block() as b:
            speE, sdvE, sacE = spe.v, sdv.v, sac.v
            ew0 = ewl.s[0].v
            xq0 = [exq.s[0].v, exq.s[1].v]
            xl0 = [exl.s[0].v, exl.s[1].v]
            ey0 = [eyl.s[0].v, eyl.s[1].v]
            eo0 = [eol.s[0].v, eol.s[1].v]
            peo = [("g", 0, i) for i in range(6)]
            for st in range(NT):
                if st < NT - 1:
                    peo += [("g", st + 1, i) for i in range(4)]
                peo += [("t", st, g) for g in range(6)]
                peo += [("d", st, n) for n in range(2)]
                if st < NT - 1:
                    peo += [("g", st + 1, i) for i in range(4, 6)]
            pepos = {t: i + 1 for i, t in enumerate(peo)}
            e_gup = lambda st, ici: speE + pepos[("g", st, ici)]
            e_tp = lambda st, g: speE + pepos[("t", st, g)]
            e_dn = lambda st, nc2: speE + pepos[("d", st, nc2)]
            aco = [("s", 0, i) for i in range(6)]
            for st in range(NT):
                if st < NT - 1:
                    aco += [("s", st + 1, i) for i in range(3)]
                aco += [("e", st, g) for g in range(6)]
                if st < NT - 1:
                    aco += [("s", st + 1, i) for i in range(3, 6)]
            acpos = {t: i + 1 for i, t in enumerate(aco)}
            e_silu = lambda st, ici: sacE + acpos[("s", st, ici)]
            e_hqev = lambda st, g: sacE + acpos[("e", st, g)]
            dvo = []
            for st in range(NT):
                dvo += [("h", st, 0), ("h", st, 1), ("h", st, 2), ("h", st, 3),
                        ("q", st, 0), ("h", st, 4), ("h", st, 5), ("q", st, 1)]
                if st >= 1:
                    dvo += [("f", st - 1, 0)]
            dvo += [("f", NT - 1, 0)]
            dvpos = {t: i + 1 for i, t in enumerate(dvo)}
            e_hev = lambda st, ici: sdvE + dvpos[("h", st, ici)]
            e_q = lambda st, half: sdvE + dvpos[("q", st, half)]
            e_fin = lambda st: sdvE + dvpos[("f", st, 0)]
            e_ld = lambda st: xq0[st % 2] + 16 * (st // 2 + 1)
            e_xld = lambda st: xl0[st % 2] + 16 * (st // 2 + 1)
            e_yld = lambda st: ey0[st % 2] + 16 * (st // 2 + 1)
            e_odma = lambda st: eo0[st % 2] + 16 * (st // 2 + 1)

            @b.sync
            def _(sync):
                sync.wait_ge(dys.s[0].h, dys.s[0].v)
                sync.wait_ge(dys.s[1].h, dys.s[1].v)
                for wi, (scr, img, KT_, Fw) in enumerate(
                        ((wgq_t, wgE, KD, I), (wuq_t, wuE, KD, I),
                         (wdq_t, wdE, KI, D))):
                    rowsz = KD * I if wi < 2 else KI * D
                    ewl.issue(sync.dma_start(
                        ap(img, 0, [[rowsz, 128], [Fw, KT_], [1, Fw]]),
                        ap(scr.tensor, 0, [[Fw, 128], [128 * Fw, KT_], [1, Fw]])))
                for st0 in (0, 1):
                    exq.issue(sync.dma_start(
                        xqld[:, (st0 % 2) * D:(st0 % 2 + 1) * D],
                        ap(xqT_d.tensor, st0 * 128,
                           [[T, 128], [128 * T, KD], [1, 128]])), st0 % 2)
                    exl.issue(sync.dma_start(
                        xtE[:, (st0 % 2) * D:(st0 % 2 + 1) * D],
                        x_d[bi, st0 * 128:(st0 + 1) * 128, :]), st0 % 2)
                sync.wait_ge(dys.s[0].h, dys.s[0].v)
                for st0 in (0, 1):
                    eyl.issue(sync.dma_start(
                        ytE[:, (st0 % 2) * D:(st0 % 2 + 1) * D],
                        ap(y_scr.tensor, st0 * 64 * D,
                           [[D, 64], [D * D, 2], [1, D]])), st0 % 2)
                for st in range(NT):
                    if st + 2 <= NT - 1:
                        sync.wait_ge(spe.h, e_gup(st, 5))
                        exq.issue(sync.dma_start(
                            xqld[:, (st % 2) * D:(st % 2 + 1) * D],
                            ap(xqT_d.tensor, (st + 2) * 128,
                               [[T, 128], [128 * T, KD], [1, 128]])), st % 2)
                    if st >= 2:
                        sync.wait_ge(sdv.h, e_fin(st - 2))
                        eol.issue(sync.dma_start(
                            out_d[bi, (st - 2) * 128:(st - 1) * 128, :],
                            outE[:, ((st - 2) % 2) * D:((st - 2) % 2 + 1) * D]),
                            st % 2)
                        eyl.issue(sync.dma_start(
                            ytE[:, (st % 2) * D:(st % 2 + 1) * D],
                            ap(y_scr.tensor, st * 64 * D,
                               [[D, 64], [D * D, 2], [1, D]])), st % 2)
                        exl.issue(sync.dma_start(
                            xtE[:, (st % 2) * D:(st % 2 + 1) * D],
                            x_d[bi, st * 128:(st + 1) * 128, :]), st % 2)
                for st in (NT - 2, NT - 1):
                    sync.wait_ge(sdv.h, e_fin(st))
                    eol.issue(sync.dma_start(
                        out_d[bi, st * 128:(st + 1) * 128, :],
                        outE[:, (st % 2) * D:(st % 2 + 1) * D]), st % 2)

            @b.tensor
            def _(t):
                t.wait_ge(ewl.s[0].h, ew0 + 48)
                for task in peo:
                    kind = task[0]
                    if kind == "g":
                        _, st, ici = task
                        icw = IC_CH[ici]
                        bank = (ici % 2) * 512
                        if ici == 0:
                            t.wait_ge(exq.s[st % 2].h, e_ld(st))
                        if ici >= 2:
                            t.wait_ge(sac.h, e_silu(st, ici - 2))
                            t.wait_ge(sdv.h, e_hev(st, ici - 2))
                        elif st >= 1:
                            t.wait_ge(sac.h, e_silu(st - 1, ici + 4))
                            t.wait_ge(sdv.h, e_hev(st - 1, ici + 4))
                        io = IC_OFF[ici]
                        for kc in range(KD):
                            t.matmul(
                                pg[:, bank:bank + icw],
                                xqld[:, (st % 2) * D + kc * 128:(st % 2) * D + (kc + 1) * 128],
                                wgE[:, kc * I + io:kc * I + io + icw],
                                start=(kc == 0), stop=(kc == KD - 1))
                        for kc in range(KD):
                            ins = t.matmul(
                                pu[:, bank:bank + icw],
                                xqld[:, (st % 2) * D + kc * 128:(st % 2) * D + (kc + 1) * 128],
                                wuE[:, kc * I + io:kc * I + io + icw],
                                start=(kc == 0), stop=(kc == KD - 1))
                        ins.then_inc(spe.h, 1)
                    elif kind == "t":
                        _, st, g = task
                        t.wait_ge(sdv.h, e_q(st, 0 if g < 3 else 1))
                        if g >= 2:
                            t.wait_ge(sac.h, e_hqev(st, g - 2))
                        elif st >= 1:
                            t.wait_ge(sac.h, e_hqev(st - 1, g + 4))
                        bank = (g % 2) * 1024
                        for kk in range(TP_N[g]):
                            blk = TP_OFF[g] + kk
                            ins = t.transpose(
                                ptE[:, bank + kk * 128:bank + (kk + 1) * 128],
                                hq[:, blk * 128:(blk + 1) * 128], ident[:])
                        ins.then_inc(spe.h, 1)
                    else:
                        _, st, nc2 = task
                        t.wait_ge(sac.h, e_hqev(st, 5))
                        if st >= 1 and nc2 == 0:
                            t.wait_ge(sdv.h, e_fin(st - 1))
                        bank = nc2 * 512
                        for kc2 in range(KI):
                            ins = t.matmul(
                                pdn[:, bank:bank + 512],
                                hqT[:, kc2 * 128:(kc2 + 1) * 128],
                                wdE[:, kc2 * D + nc2 * 512:kc2 * D + (nc2 + 1) * 512],
                                start=(kc2 == 0), stop=(kc2 == KI - 1))
                        ins.then_inc(spe.h, 1)
            spe.v += 14 * NT

            @b.scalar
            def _(s):
                for task in aco:
                    kind = task[0]
                    if kind == "s":
                        _, st, ici = task
                        icw = IC_CH[ici]
                        bank = (ici % 2) * 512
                        s.wait_ge(spe.h, e_gup(st, ici))
                        if ici == 0 and st >= 1:
                            s.wait_ge(sdv.h, e_q(st - 1, 0))
                        if ici == 3 and st >= 1:
                            s.wait_ge(sdv.h, e_q(st - 1, 1))
                        half = 1 if ici >= 3 else 0
                        hhh = hh1 if half else hh0
                        coff = IC_OFF[ici] - half * HHW
                        s.activation(hhh[:, coff:coff + icw],
                                     pg[:, bank:bank + icw], AF.Silu
                                     ).then_inc(sac.h, 1)
                    else:
                        _, st, g = task
                        s.wait_ge(spe.h, e_tp(st, g))
                        if g == 5:
                            s.drain()
                        bank = (g % 2) * 1024
                        n = TP_N[g]
                        s.copy(
                            hqT[:, TP_OFF[g] * 128:TP_OFF[g] * 128 + n * 128],
                            ptE[:, bank:bank + n * 128]).then_inc(sac.h, 1)
            sac.v += 12 * NT

            @b.vector
            def _(v):
                def do_fin(st):
                    v.wait_ge(spe.h, e_dn(st, 1))
                    v.wait_ge(exl.s[st % 2].h, e_xld(st))
                    v.wait_ge(eyl.s[st % 2].h, e_yld(st))
                    if st >= 2:
                        v.wait_ge(eol.s[st % 2].h, e_odma(st - 2))
                    ob = (st % 2) * D
                    for nc2 in range(2):
                        bank = nc2 * 512
                        v.tensor_tensor(
                            outE[:, ob + nc2 * 512:ob + (nc2 + 1) * 512],
                            xtE[:, ob + nc2 * 512:ob + (nc2 + 1) * 512],
                            pdn[:, bank:bank + 512], ALU.add)
                        if nc2 == 1:
                            v.drain()
                        ins = v.tensor_tensor(
                            outE[:, ob + nc2 * 512:ob + (nc2 + 1) * 512],
                            outE[:, ob + nc2 * 512:ob + (nc2 + 1) * 512],
                            ytE[:, ob + nc2 * 512:ob + (nc2 + 1) * 512], ALU.add)
                    ins.then_inc(sdv.h, 1)

                for task in dvo:
                    kind = task[0]
                    if kind == "h":
                        _, st, ici = task
                        icw = IC_CH[ici]
                        bank = (ici % 2) * 512
                        v.wait_ge(sac.h, e_silu(st, ici))
                        half_ = 1 if ici >= 3 else 0
                        hhh = hh1 if half_ else hh0
                        coff = IC_OFF[ici] - half_ * HHW
                        v.tensor_tensor(hhh[:, coff:coff + icw],
                                        hhh[:, coff:coff + icw],
                                        pu[:, bank:bank + icw], ALU.mult
                                        ).then_inc(sdv.h, 1)
                    elif kind == "q":
                        _, st, half = task
                        if st >= 1:
                            v.wait_ge(spe.h, e_tp(st - 1, 2 if half == 0 else 5))
                        v.drain()
                        quant(v, (hh1 if half else hh0)[:],
                              hq[:, half * HHW:(half + 1) * HHW], HHW)
                        v.drain()
                        v.nop().then_inc(sdv.h, 1)
                    else:
                        do_fin(task[1])
            sdv.v += 9 * NT

    with nc.Block() as b:
        @b.sync
        def _(sync):
            for d in ALL_DS:
                for s in d.s:
                    sync.wait_ge(s.h, s.v)

    nc.finalize()
    es.close()
    return nc


_CACHE = {}


def kernel(L0_attn_x, L0_attn_Wq, L0_attn_Wk, L0_attn_Wv, L0_attn_Wp,
           L0_ffn_w_gate, L0_ffn_w_up, L0_ffn_w_down):
    if "nc" not in _CACHE:
        _CACHE["nc"] = build_program()
    nc = _CACHE["nc"]
    x = np.ascontiguousarray(np.asarray(L0_attn_x, dtype=np.float32))
    common = {
        "Wq": np.ascontiguousarray(np.asarray(L0_attn_Wq, np.float32)),
        "Wk": np.ascontiguousarray(np.asarray(L0_attn_Wk, np.float32)),
        "Wv": np.ascontiguousarray(np.asarray(L0_attn_Wv, np.float32)),
        "Wp": np.ascontiguousarray(np.asarray(L0_attn_Wp, np.float32)),
        "wg": np.ascontiguousarray(np.asarray(L0_ffn_w_gate, np.float32)),
        "wu": np.ascontiguousarray(np.asarray(L0_ffn_w_up, np.float32)),
        "wd": np.ascontiguousarray(np.asarray(L0_ffn_w_down, np.float32)),
    }
    in_maps = [dict(common, x=x[c * NBATCH:(c + 1) * NBATCH]) for c in range(8)]
    res = run_bass_kernel_spmd(nc, in_maps, core_ids=list(range(8)))
    return np.concatenate([r["out"] for r in res.results], axis=0)
